# version 19
# baseline (speedup 1.0000x reference)
import math
import sys
from contextlib import ExitStack

import numpy as np

try:
    import concourse.bass as bass
    import concourse.mybir as mybir
except ImportError:
    sys.path.insert(0, "/opt/trn_rl_repo")
    import concourse.bass as bass
    import concourse.mybir as mybir

F32 = mybir.dt.float32
BF16 = mybir.dt.bfloat16
I32 = mybir.dt.int32
ALU = mybir.AluOpType
AF = mybir.ActivationFunctionType
AX = mybir.AxisListType.X

LN_EPS = 1e-5


class Cfg:
    def __init__(self, T_OWN=2048, T_FULL=4096, D=2048, NH=16, HID=8192, R=16):
        self.T_OWN, self.T_FULL, self.D, self.NH, self.HID, self.R = T_OWN, T_FULL, D, NH, HID, R
        self.DH = 128
        assert self.NH * self.DH == self.D
        self.DC = D // 128
        self.OG = D // 512
        self.HC = HID // 128
        self.HG = HID // 512
        self.TCO = T_OWN // 128
        self.TCF = T_FULL // 128
        self.QT = T_OWN // 512
        self.KC = T_FULL // 128
        self.SCL = self.DH ** -0.5


class Prog:
    ENGS = ("g", "v", "t", "s", "y")

    def __init__(self, nc, stack):
        self.nc = nc
        self.ops = {e: [] for e in self.ENGS}
        self.cnt = {}
        self.sem = {}
        self.stack = stack

    def mksem(self, *names):
        for n in names:
            self.sem[n] = self.stack.enter_context(self.nc.semaphore(n))
            self.cnt[n] = 0

    def fam(self, base, nbuf):
        names = [f"{base}{i}" for i in range(nbuf)]
        self.mksem(*names)
        return names

    def em(self, eng, fn):
        self.ops[eng].append(fn)

    def wait(self, eng, sem, value):
        if value is None or value <= 0:
            return
        self.em(eng, lambda e, s=self.sem[sem], v=value: e.wait_ge(s, v))

    def inc(self, sem, n):
        self.cnt[sem] += n
        return self.cnt[sem]

    def val(self, sem):
        return self.cnt[sem]

    def dma(self, eng, outap, inap, sem="dg", **kw):
        v = self.inc(sem, 16)
        so = self.sem[sem]
        self.em(eng, lambda e, o=outap, i=inap, s=so, kw=kw:
                e.dma_start(out=o, in_=i, **kw).then_inc(s, 16))
        return v

    def dma_chunked(self, eng, dst, srcdram, sem, p, inner, nchunks, max_desc=1024):
        per = max(1, max_desc // 128)
        v = None
        for c0 in range(0, nchunks, per):
            c1 = min(nchunks, c0 + per)
            v = self.dma(eng,
                         dst[:, c0 * inner:c1 * inner].rearrange("p (c d) -> p c d", d=inner),
                         srcdram[c0 * 128:c1 * 128, :].rearrange("(c p) d -> p c d", p=128),
                         sem=sem)
        return v

    def dmaT(self, outap, inap, sem="dy"):
        v = self.inc(sem, 16)
        so = self.sem[sem]
        self.em("y", lambda e, o=outap, i=inap, s=so:
                e.dma_start_transpose(out=o, in_=i).then_inc(s, 16))
        return v

    def ein(self, eng, fn, sem, n=1):
        v = self.inc(sem, n)
        so = self.sem[sem]
        self.em(eng, lambda e, f=fn, s=so, k=n: f(e).then_inc(s, k))
        return v

    def barrier(self):
        vals = {k: v for k, v in self.cnt.items() if v > 0}
        for eng in self.ENGS:
            for k, v in vals.items():
                self.wait(eng, k, v)

    def emit(self):
        with self.nc.Block() as block:
            @block.gpsimd
            def _(e):
                for fn in self.ops["g"]:
                    fn(e)

            @block.vector
            def _(e):
                for fn in self.ops["v"]:
                    fn(e)

            @block.tensor
            def _(e):
                for fn in self.ops["t"]:
                    fn(e)

            @block.scalar
            def _(e):
                for fn in self.ops["s"]:
                    fn(e)

            @block.sync
            def _(e):
                for fn in self.ops["y"]:
                    fn(e)


def build_nc(cfg: Cfg, debug=(), stop_after=99):
    nc = bass.Bass("TRN2", target_bir_lowering=False)
    C = cfg
    dk = lambda name: "ExternalOutput" if name in debug else "Internal"

    x_full = nc.dram_tensor("x_full", [C.T_FULL, C.D], F32, kind="ExternalInput")
    wq_t = nc.dram_tensor("wq_t", [C.D, C.D], F32, kind="ExternalInput")
    wk_t = nc.dram_tensor("wk_t", [C.D, C.D], F32, kind="ExternalInput")
    wv_t = nc.dram_tensor("wv_t", [C.D, C.D], F32, kind="ExternalInput")
    wo_t = nc.dram_tensor("wo_t", [C.D, C.D], F32, kind="ExternalInput")
    qb_r = nc.dram_tensor("qb_r", [1, C.D], F32, kind="ExternalInput")
    kb_r = nc.dram_tensor("kb_r", [1, C.D], F32, kind="ExternalInput")
    vb_r = nc.dram_tensor("vb_r", [1, C.D], F32, kind="ExternalInput")
    ob_r = nc.dram_tensor("ob_r", [1, C.D], F32, kind="ExternalInput")
    cq_in = nc.dram_tensor("cq_in", [C.T_OWN, 128], BF16, kind="ExternalInput")
    sq_in = nc.dram_tensor("sq_in", [C.T_OWN, 128], BF16, kind="ExternalInput")
    ck_in = nc.dram_tensor("ck_in", [C.T_FULL, 128], BF16, kind="ExternalInput")
    sk_in = nc.dram_tensor("sk_in", [C.T_FULL, 128], BF16, kind="ExternalInput")
    wupq = nc.dram_tensor("wupq", [C.HID, C.D], I32, kind="ExternalInput")
    wups = nc.dram_tensor("wups", [C.HID, C.D // 64], F32, kind="ExternalInput")
    bup_t = nc.dram_tensor("bup_t", [128, C.HC], F32, kind="ExternalInput")
    wdnq_t = nc.dram_tensor("wdnq_t", [C.HID, C.D], I32, kind="ExternalInput")
    wdnsc_t = nc.dram_tensor("wdnsc_t", [C.HID // 64, C.D], F32, kind="ExternalInput")
    bdn_r = nc.dram_tensor("bdn_r", [1, C.D], F32, kind="ExternalInput")
    a_up = nc.dram_tensor("a_up", [C.D, C.R], F32, kind="ExternalInput")
    b_upl = nc.dram_tensor("b_upl", [C.R, C.HID], F32, kind="ExternalInput")
    a_dn = nc.dram_tensor("a_dn", [C.HID, C.R], F32, kind="ExternalInput")
    b_dnl = nc.dram_tensor("b_dnl", [C.R, C.D], F32, kind="ExternalInput")
    ident = nc.dram_tensor("ident", [128, 128], F32, kind="ExternalInput")

    out = nc.dram_tensor("out", [C.T_OWN, C.D], F32, kind="ExternalOutput")

    ln1x_d = nc.dram_tensor("ln1x_d", [C.T_FULL, C.D], BF16, kind=dk("ln1x_d"))
    q_d = nc.dram_tensor("q_d", [C.T_OWN, C.D], BF16, kind=dk("q_d"))
    k_d = nc.dram_tensor("k_d", [C.T_FULL, C.D], BF16, kind=dk("k_d"))
    v_d = nc.dram_tensor("v_d", [C.T_FULL, C.D], BF16, kind=dk("v_d"))
    cneg_d = nc.dram_tensor("cneg_d", [C.NH, 1], F32, kind=dk("cneg_d"))
    ctx_d = nc.dram_tensor("ctx_d", [C.NH * 128, C.T_OWN], BF16, kind=dk("ctx_d"))
    xnew_d = nc.dram_tensor("xnew_d", [C.T_OWN, C.D], F32, kind=dk("xnew_d"))
    ln2x_d = nc.dram_tensor("ln2x_d", [C.T_OWN, C.D], BF16, kind=dk("ln2x_d"))
    wup_deq_d = nc.dram_tensor("wup_deq_d", [C.HID, C.D], BF16, kind=dk("wup_deq_d"))
    wdn_deq_d = nc.dram_tensor("wdn_deq_d", [C.HID, C.D], BF16, kind=dk("wdn_deq_d"))
    h2_d = nc.dram_tensor("h2_d", [C.HID, C.T_OWN], BF16, kind=dk("h2_d"))
    y2_d = nc.dram_tensor("y2_d", [C.T_OWN, C.D], F32, kind=dk("y2_d"))
    linv_d = nc.dram_tensor("linv_d", [4, 512], F32, kind="Internal")
    nred_d = nc.dram_tensor("nred_d", [2 * C.NH, 1], F32, kind="Internal")

    st = ExitStack()
    P = Prog(nc, st)
    P.mksem("dg", "dy", "dO", "dX", "vS", "sS", "tM", "sE", "vE", "pV", "hD", "vQ", "vR", "sQ", "dF")

    eps_t = st.enter_context(nc.sbuf_tensor("eps_t", [128, 1], F32))
    zero_t = st.enter_context(nc.sbuf_tensor("zero_t", [128, 1], F32))
    ones_bf = st.enter_context(nc.sbuf_tensor("ones_bf", [128, 1], BF16))
    ident_sb = st.enter_context(nc.sbuf_tensor("ident_sb", [128, 128], F32))
    bexp_all = st.enter_context(nc.sbuf_tensor("bexp_all", [128, C.NH], F32))

    P.em("v", lambda e: e.memset(eps_t[:], LN_EPS))
    P.em("v", lambda e: e.memset(zero_t[:], 0.0))
    P.em("v", lambda e: e.memset(ones_bf[:], 1.0))
    P.dma("g", ident_sb[:], ident[:])

    def ln_chain(xap, outap, s24, m2, s1, r1, n1):
        def vstats(e):
            for j in range(C.OG):
                e.bn_stats(s24[:, j * 6:(j + 1) * 6], xap[:, j * 512:(j + 1) * 512])
            e.drain()
            e.bn_aggr(m2[:], s24[:])
            return e.drain()
        P.ein("v", vstats, "vS")
        P.wait("s", "vS", P.val("vS"))
        P.ein("s", lambda e: e.activation(s1[:], m2[:, 1:2], AF.Sqrt,
                                          bias=eps_t[:, 0:1], scale=1.0), "sS")
        P.wait("v", "sS", P.val("sS"))
        def vfin(e):
            e.reciprocal(r1[:], s1[:])
            e.drain()
            e.tensor_scalar(n1[:], m2[:, 0:1], r1[:, 0:1], -1.0, ALU.mult, ALU.mult)
            e.drain()
            e.tensor_scalar(outap, xap, r1[:, 0:1], n1[:, 0:1], ALU.mult, ALU.add)
            return e.drain()
        return P.ein("v", vfin, "vS")

    NP1 = 4
    p1 = ExitStack()
    xb = [p1.enter_context(nc.sbuf_tensor(f"xb{i}", [128, C.D], F32)) for i in range(NP1)]
    lnb = [p1.enter_context(nc.sbuf_tensor(f"lnb{i}", [128, C.D], BF16)) for i in range(NP1)]
    s24 = [p1.enter_context(nc.sbuf_tensor(f"s24_{i}", [128, 6 * C.OG], F32)) for i in range(NP1)]
    mv2 = [p1.enter_context(nc.sbuf_tensor(f"mv2_{i}", [128, 2], F32)) for i in range(NP1)]
    sd1 = [p1.enter_context(nc.sbuf_tensor(f"sd1_{i}", [128, 1], F32)) for i in range(NP1)]
    rs1 = [p1.enter_context(nc.sbuf_tensor(f"rs1_{i}", [128, 1], F32)) for i in range(NP1)]
    nb1 = [p1.enter_context(nc.sbuf_tensor(f"nb1_{i}", [128, 1], F32)) for i in range(NP1)]

    fx = P.fam("fx", NP1)
    fln = P.fam("fln", NP1)
    apply_done = {}
    store_done = {}
    stat_done = {}
    sqrt_done = {}

    def p1_stage1(tc):
        p = tc % NP1
        if tc >= NP1:
            P.wait("y", "vS", apply_done[tc - NP1])
        dv = P.dma("y", xb[p][:], x_full[tc * 128:(tc + 1) * 128, :], sem=fx[p])
        P.wait("v", fx[p], dv)
        def vstats(e, p=p):
            for j in range(C.OG):
                e.bn_stats(s24[p][:, j * 6:(j + 1) * 6], xb[p][:, j * 512:(j + 1) * 512])
            e.drain()
            e.bn_aggr(mv2[p][:], s24[p][:])
            return e.drain()
        stat_done[tc] = P.ein("v", vstats, "vS")
        P.wait("s", "vS", stat_done[tc])
        sqrt_done[tc] = P.ein("s", lambda e, p=p: e.activation(
            sd1[p][:], mv2[p][:, 1:2], AF.Sqrt, bias=eps_t[:, 0:1], scale=1.0), "sS")

    def p1_stage2(tc):
        p = tc % NP1
        P.wait("v", "sS", sqrt_done[tc])
        if tc >= NP1:
            P.wait("v", fln[p], store_done[tc - NP1])
        def vfin(e, p=p):
            e.reciprocal(rs1[p][:], sd1[p][:])
            e.drain()
            e.tensor_scalar(nb1[p][:], mv2[p][:, 0:1], rs1[p][:, 0:1], -1.0, ALU.mult, ALU.mult)
            e.drain()
            e.tensor_scalar(lnb[p][:], xb[p][:], rs1[p][:, 0:1], nb1[p][:, 0:1], ALU.mult, ALU.add)
            return e.drain()
        apply_done[tc] = P.ein("v", vfin, "vS")
        P.wait("s", "vS", apply_done[tc])
        store_done[tc] = P.dma("s", ln1x_d[tc * 128:(tc + 1) * 128, :], lnb[p][:], sem=fln[p])

    for tc in range(C.TCF):
        p1_stage1(tc)
        if tc >= 2:
            p1_stage2(tc - 2)
    p1_stage2(C.TCF - 2)
    p1_stage2(C.TCF - 1)
    if stop_after < 2:
        p1.close()
        P.barrier()
        return nc, P, st

    p2 = ExitStack()
    wfull = p2.enter_context(nc.sbuf_tensor("wfull", [128, C.DC * C.D], BF16))
    ltb = [p2.enter_context(nc.sbuf_tensor(f"ltb{i}", [128, C.DC * 512], BF16)) for i in range(2)]
    bias_bc = p2.enter_context(nc.sbuf_tensor("bias_bc", [128, C.D], F32))
    rbuf = [p2.enter_context(nc.sbuf_tensor(f"rbuf{i}", [128, 512], BF16)) for i in range(3)]
    tmpa = p2.enter_context(nc.sbuf_tensor("tmpa", [128, 512], BF16))
    tmpb = p2.enter_context(nc.sbuf_tensor("tmpb", [128, 512], BF16))
    obuf = [p2.enter_context(nc.sbuf_tensor(f"obuf{i}", [128, 512], BF16)) for i in range(3)]
    sqb = [p2.enter_context(nc.sbuf_tensor(f"sqb{i}", [128, 512], F32)) for i in range(2)]
    ckt = p2.enter_context(nc.sbuf_tensor("ckt", [128, C.TCF * 128], BF16))
    skt = p2.enter_context(nc.sbuf_tensor("skt", [128, C.TCF * 128], BF16))
    cqt = p2.enter_context(nc.sbuf_tensor("cqt", [128, C.TCO * 128], BF16))
    sqt = p2.enter_context(nc.sbuf_tensor("sqt", [128, C.TCO * 128], BF16))
    nsq_q = p2.enter_context(nc.sbuf_tensor("nsq_q", [128, C.NH * C.TCO], F32))
    nsq_k = p2.enter_context(nc.sbuf_tensor("nsq_k", [128, C.NH * C.TCF], F32))
    pqk = [p2.enter_context(nc.psum_tensor(f"pqk{i}", [128, 512], F32)) for i in range(4)]

    dtab = None
    for (dst, src) in ((ckt, ck_in), (skt, sk_in), (cqt, cq_in), (sqt, sq_in)):
        dtab = P.dma_chunked("g", dst, src, "dg", 128, 128, src.shape[0] // 128, max_desc=512)
    P.em("v", lambda e: e.memset(nsq_q[:], 0.0))
    P.em("v", lambda e: e.memset(nsq_k[:], 0.0))
    P.wait("v", "dg", dtab)

    PASSES = [
        ("k", wk_t, kb_r, k_d, C.TCF, True, ckt, skt, nsq_k, C.TCF),
        ("v", wv_t, vb_r, v_d, C.TCF, False, None, None, None, 0),
        ("q", wq_t, qb_r, q_d, C.TCO, True, cqt, sqt, nsq_q, C.TCO),
    ]

    uglob = 0
    evac_done = {}
    rope_done = {}
    sq_done = {}
    nsq_done = {}
    out_done = {}
    t5_mm_end = {}
    hpg = 512 // C.DH

    def emit_nsq(u0, og0, tc0, nsq0, NSQT0):
        P.wait("v", "sQ", sq_done[u0])
        def nr(e, og=og0, tc=tc0, nsq=nsq0, NSQT=NSQT0, sb=sqb[u0 % 2]):
            o3 = nsq[:].rearrange("p (h t) -> p h t", t=NSQT)
            e.reduce_sum(o3[:, og * hpg:(og + 1) * hpg, tc],
                         sb[:].rearrange("p (h d) -> p h d", d=128), AX)
            return e.drain()
        nsq_done[u0] = P.ein("v", nr, "vE")

    pending_nsq = {}
    last_nsq_flush = [0]
    flt = P.fam("flt", 2)
    fw2 = P.fam("fw2", 1)
    fqv = P.fam("fqv", 3)
    for pi, (pname, wsrc, bsrc, dsth, TC, do_rope, ctab, stab, nsq, NSQT) in enumerate(PASSES):
        if pi > 0:
            P.wait("g", "tM", P.val("tM"))
            P.wait("g", "vQ", P.val("vQ"))
        if do_rope and last_nsq_flush[0] > 0:
            P.wait("s", "vE", last_nsq_flush[0])
        P.dma_chunked("g", wfull, wsrc, fw2[0], 128, C.D, C.DC, max_desc=512)
        db = P.dma("g", bias_bc[:], bsrc[:].broadcast_to([128, C.D]), sem=fw2[0])
        P.wait("t", fw2[0], db)
        P.wait("v", fw2[0], db)
        n512 = TC // 4
        pass_t5_base = P.val("tM")
        for t5 in range(n512):
            lp = t5 % 2
            if (pi, t5 - 2) in t5_mm_end:
                P.wait("y", "tM", t5_mm_end[(pi, t5 - 2)])
            elif t5 < 2 and pi > 0:
                P.wait("y", "tM", pass_t5_base)
            for j in range(4):
                tile = t5 * 4 + j
                P.wait("y", fln[tile % NP1], store_done[tile])
            dtv = None
            for dc in range(C.DC):
                dtv = P.dmaT(ltb[lp][:, dc * 512:(dc + 1) * 512],
                             ln1x_d[t5 * 512:(t5 + 1) * 512, dc * 128:(dc + 1) * 128],
                             sem=flt[lp])
            P.wait("t", flt[lp], dtv)
            for t1 in range(4):
                tc = t5 * 4 + t1
                for og in range(C.OG):
                    u = uglob
                    uglob += 1
                    pp = pqk[u % 4]
                    bufi = u % 3
                    if u - 4 in evac_done:
                        P.wait("t", "vQ", evac_done[u - 4])
                    def mmg(e, lp=lp, t1=t1, og=og, pp=pp):
                        last = None
                        for dc in range(C.DC):
                            last = e.matmul(
                                pp[:],
                                ltb[lp][:, dc * 512 + t1 * 128: dc * 512 + (t1 + 1) * 128],
                                wfull[:, dc * C.D + og * 512: dc * C.D + (og + 1) * 512],
                                start=(dc == 0), stop=(dc == C.DC - 1))
                        return last
                    mmv = P.ein("t", mmg, "tM")
                    P.wait("v", "tM", mmv)
                    rb = rbuf[bufi] if do_rope else obuf[bufi]
                    if u - 3 in out_done:
                        P.wait("v", fqv[(u - 3) % 3], out_done[u - 3])
                    if not do_rope and u - 3 in sq_done:
                        P.wait("v", "sQ", sq_done[u - 3])
                    evac_done[u] = P.ein("v", lambda e, pp=pp, rb=rb, og=og: e.tensor_tensor(
                        rb[:], pp[:], bias_bc[:, og * 512:(og + 1) * 512], ALU.add), "vQ")
                    if do_rope:
                        ob = obuf[bufi]
                        if u - 3 in sq_done:
                            P.wait("v", "sQ", sq_done[u - 3])
                        def ropec(e, rb=rb, ob=ob, tc=tc, ctab=ctab, stab=stab):
                            e.drain()
                            ctile = ctab[:, tc * 128:(tc + 1) * 128].rearrange(
                                "p (r d) -> p r d", r=1).broadcast_to([128, hpg, 128])
                            s0 = stab[:, tc * 128:tc * 128 + 64].rearrange(
                                "p (r d) -> p r d", r=1).broadcast_to([128, hpg, 64])
                            s1 = stab[:, tc * 128 + 64:(tc + 1) * 128].rearrange(
                                "p (r d) -> p r d", r=1).broadcast_to([128, hpg, 64])
                            r3 = rb[:].rearrange("p (c d) -> p c d", d=128)
                            r4 = rb[:].rearrange("p (c two d) -> p c two d", two=2, d=64)
                            ta3 = tmpa[:].rearrange("p (c d) -> p c d", d=128)
                            tb4 = tmpb[:].rearrange("p (c two d) -> p c two d", two=2, d=64)
                            e.tensor_tensor(ta3, r3, ctile, ALU.mult)
                            e.tensor_tensor(tb4[:, :, 0, :], r4[:, :, 1, :], s0, ALU.mult)
                            e.tensor_tensor(tb4[:, :, 1, :], r4[:, :, 0, :], s1, ALU.mult)
                            e.drain()
                            e.tensor_tensor(ob[:], tmpa[:], tmpb[:], ALU.add)
                            return e.drain()
                        rope_done[u] = P.ein("v", ropec, "vR")
                        if u - 2 in pending_nsq:
                            emit_nsq(*pending_nsq.pop(u - 2))
                            P.wait("s", "vE", nsq_done[u - 2])
                        P.wait("s", "vR", rope_done[u])
                        sq_done[u] = P.ein("s", lambda e, ob=ob, sb=sqb[u % 2]:
                                           e.activation(sb[:], ob[:], AF.Square), "sQ")
                        pending_nsq[u] = (u, og, tc, nsq, NSQT)
                        P.wait("g", "vR", rope_done[u])
                    else:
                        ob = rb
                        P.wait("g", "vQ", evac_done[u])
                    out_done[u] = P.dma("g", dsth[tc * 128:(tc + 1) * 128,
                                                 og * 512:(og + 1) * 512], ob[:], sem=fqv[u % 3])

            t5_mm_end[(pi, t5)] = P.val("tM")
        for u0 in sorted(pending_nsq):
            emit_nsq(*pending_nsq[u0])
        pending_nsq.clear()
        last_nsq_flush[0] = P.val("vE")

    fin = ExitStack()
    nmax = fin.enter_context(nc.sbuf_tensor("nmax", [128, 2 * C.NH], F32))
    ntp = fin.enter_context(nc.psum_tensor("ntp", [2 * C.NH, 128], F32))
    nfin = fin.enter_context(nc.sbuf_tensor("nfin", [2 * C.NH, 128], F32))
    nred = fin.enter_context(nc.sbuf_tensor("nred", [2 * C.NH, 1], F32))
    nsqt2 = fin.enter_context(nc.sbuf_tensor("nsqt2", [C.NH, 2], F32))
    cneg = fin.enter_context(nc.sbuf_tensor("cneg", [C.NH, 1], F32))

    def nmaxc(e):
        e.reduce_max(nmax[:, 0:C.NH], nsq_q[:].rearrange("p (h t) -> p h t", t=C.TCO), AX)
        e.reduce_max(nmax[:, C.NH:2 * C.NH], nsq_k[:].rearrange("p (h t) -> p h t", t=C.TCF), AX)
        return e.drain()
    P.ein("v", nmaxc, "vE")
    P.wait("t", "vE", P.val("vE"))
    P.ein("t", lambda e: e.transpose(ntp[:], nmax[:], ident_sb[:]), "tM")
    P.wait("v", "tM", P.val("tM"))
    def cnc(e):
        e.tensor_copy(nfin[:], ntp[:])
        e.drain()
        e.reduce_max(nred[:], nfin[:], AX)
        return e.drain()
    P.ein("v", cnc, "vE")
    fcn = P.fam("fcn", 1)
    P.wait("g", "vE", P.val("vE"))
    dnr = P.dma("g", nred_d[:, :], nred[:], sem=fcn[0])
    P.wait("g", fcn[0], dnr)
    P.dma("g", nsqt2[:, 0:1], nred_d[0:C.NH, :], sem=fcn[0])
    dnl = P.dma("g", nsqt2[:, 1:2], nred_d[C.NH:2 * C.NH, :], sem=fcn[0])
    P.wait("v", fcn[0], dnl)
    def cn2a(e):
        e.tensor_tensor(cneg[:], nsqt2[:, 0:1], nsqt2[:, 1:2], ALU.mult)
        return e.drain()
    P.ein("v", cn2a, "vE")
    P.wait("s", "vE", P.val("vE"))
    P.ein("s", lambda e: e.activation(cneg[:], cneg[:], AF.Sqrt,
                                      bias=zero_t[0:C.NH, 0:1], scale=1.0), "sS")
    P.wait("v", "sS", P.val("sS"))
    def cn2b(e):
        e.tensor_scalar(cneg[:], cneg[:], -1.01, -0.01, ALU.mult, ALU.add)
        return e.drain()
    P.ein("v", cn2b, "vE")
    P.wait("g", "vE", P.val("vE"))
    dcw = P.dma("g", cneg_d[:, :], cneg[:], sem=fcn[0])
    P.wait("g", fcn[0], dcw)
    for h in range(C.NH):
        P.dma("g", bexp_all[:, h:h + 1], cneg_d[h:h + 1, 0:1].broadcast_to([128, 1]), sem=fcn[0])
    fin.close()
    p2.close()
    p1.close()
    P.barrier()
    if stop_after < 3:
        return nc, P, st

    p34 = ExitStack()
    ctxT = [p34.enter_context(nc.sbuf_tensor(f"ctxT{h}", [128, C.T_OWN], BF16))
            for h in range(C.NH)]

    p5b = ExitStack()
    iqb5 = [p5b.enter_context(nc.sbuf_tensor(f"iqb5_{i}", [128, C.D], I32)) for i in range(2)]
    scb5 = [p5b.enter_context(nc.sbuf_tensor(f"scb5_{i}", [128, C.D // 64], F32)) for i in range(2)]
    sbc5 = [p5b.enter_context(nc.sbuf_tensor(f"sbc5_{i}", [128, C.D], F32)) for i in range(2)]
    dqb5 = [p5b.enter_context(nc.sbuf_tensor(f"dqb5_{i}", [128, C.D], BF16)) for i in range(2)]
    P.mksem("vD")
    fdl = P.fam("fdl", 2)
    fds = P.fam("fds", 2)
    dq_chunks = [("up", rc) for rc in range(C.HC)] + [("dn", hc) for hc in range(C.HC)]
    dq_state = {"i": 0, "store": {}, "deq": {}}

    def emit_dequant(n):
        for _ in range(n):
            i = dq_state["i"]
            if i >= len(dq_chunks):
                return
            dq_state["i"] = i + 1
            kind, rc = dq_chunks[i]
            p = i % 2
            if i - 2 in dq_state["store"]:
                P.wait("g", fds[p], dq_state["store"][i - 2])
                P.wait("v", fds[p], dq_state["store"][i - 2])
            if kind == "up":
                P.dma("g", iqb5[p][:], wupq[rc * 128:(rc + 1) * 128, :], sem=fdl[p])
                ds = P.dma("g", scb5[p][:], wups[rc * 128:(rc + 1) * 128, :], sem=fdl[p])
                P.wait("v", fdl[p], ds)
                def dqc(e, p=p):
                    q3 = iqb5[p][:].rearrange("p (c d) -> p c d", d=64)
                    o3 = dqb5[p][:].rearrange("p (c d) -> p c d", d=64)
                    s3 = scb5[p][:].rearrange("p (c d) -> p c d", d=1).broadcast_to(
                        [128, C.D // 64, 64])
                    e.scalar_tensor_tensor(o3, q3, -7.5, s3, ALU.add, ALU.mult)
                    return e.drain()
                dq_state["deq"][i] = P.ein("v", dqc, "vD")
                P.wait("y", "vD", dq_state["deq"][i])
                dq_state["store"][i] = P.dma(
                    "y", wup_deq_d[rc * 128:(rc + 1) * 128, :], dqb5[p][:], sem=fds[p])
            else:
                P.dma("g", iqb5[p][:], wdnq_t[rc * 128:(rc + 1) * 128, :], sem=fdl[p])
                P.dma("g", sbc5[p][0:64, :],
                      wdnsc_t[2 * rc:2 * rc + 1, :].broadcast_to([64, C.D]), sem=fdl[p])
                ds = P.dma("g", sbc5[p][64:128, :],
                           wdnsc_t[2 * rc + 1:2 * rc + 2, :].broadcast_to([64, C.D]), sem=fdl[p])
                P.wait("v", fdl[p], ds)
                def dqc2(e, p=p):
                    e.scalar_tensor_tensor(dqb5[p][:], iqb5[p][:], -7.5, sbc5[p][:],
                                           ALU.add, ALU.mult)
                    return e.drain()
                dq_state["deq"][i] = P.ein("v", dqc2, "vD")
                P.wait("y", "vD", dq_state["deq"][i])
                dq_state["store"][i] = P.dma(
                    "y", wdn_deq_d[rc * 128:(rc + 1) * 128, :], dqb5[p][:], sem=fds[p])
    p3 = ExitStack()
    ktb = [p3.enter_context(nc.sbuf_tensor(f"ktb{i}", [128, C.T_FULL], BF16)) for i in range(2)]
    qtb = [p3.enter_context(nc.sbuf_tensor(f"qtb{i}", [128, C.T_OWN], BF16)) for i in range(2)]
    vtb = [p3.enter_context(nc.sbuf_tensor(f"vtb{i}", [128, C.T_FULL], BF16)) for i in range(2)]
    pbuf = [p3.enter_context(nc.sbuf_tensor(f"pbuf{i}", [128, 1024], BF16)) for i in range(2)]
    l_sb = [p3.enter_context(nc.sbuf_tensor(f"l_sb{i}", [1, 512], F32)) for i in range(2)]
    linv_sb = [p3.enter_context(nc.sbuf_tensor(f"linv_sb{i}", [1, 512], F32)) for i in range(2)]
    linv_bc = [p3.enter_context(nc.sbuf_tensor(f"linv_bc{i}", [128, 512], F32)) for i in range(2)]
    p3ps = ExitStack()
    ps_s = [p3ps.enter_context(nc.psum_tensor(f"ps_s{i}", [128, 1024], F32)) for i in range(2)]
    ps_c = [p3ps.enter_context(nc.psum_tensor(f"ps_c{i}", [128, 512], F32)) for i in range(2)]
    ps_l = [p3ps.enter_context(nc.psum_tensor(f"ps_l{i}", [1, 512], F32)) for i in range(2)]

    NPAIR = C.KC // 2
    fkq = P.fam("fkq", 2)
    fv3 = P.fam("fv3", 2)
    flv = P.fam("flv", 2)
    flb = P.fam("flb", 2)

    def head_loads(h, hb):
        hs = slice(h * 128, (h + 1) * 128)
        dyv = None
        for j0 in range(0, C.T_FULL, 1024):
            w = min(1024, C.T_FULL - j0)
            dyv = P.dmaT(ktb[hb][:, j0:j0 + w], k_d[j0:j0 + w, hs], sem=fkq[hb])
        for j0 in range(0, C.T_OWN, 1024):
            w = min(1024, C.T_OWN - j0)
            dyv = P.dmaT(qtb[hb][:, j0:j0 + w], q_d[j0:j0 + w, hs], sem=fkq[hb])
        dgv = P.dma_chunked("g", vtb[hb], v_d[:, hs], fv3[hb], 128, 128, C.KC, max_desc=512)
        return dyv, dgv

    qtglob = 0
    ctx_evac_done = {}
    loads = [head_loads(0, 0)]
    for h in range(C.NH):
        hb = h % 2
        dyv, dgv = loads[h]
        P.wait("t", fkq[hb], dyv)
        P.wait("t", fv3[hb], dgv)
        if h + 1 < C.NH:
            if h >= 1:
                P.wait("y", "hD", h)
                P.wait("g", "hD", h)
            loads.append(head_loads(h + 1, (h + 1) % 2))
        for qt in range(C.QT):
            cpar = qtglob % 2
            qs = slice(qt * 512, (qt + 1) * 512)
            if qtglob - 2 in ctx_evac_done:
                P.wait("t", "vE", ctx_evac_done[qtglob - 2])
            base_se = P.val("sE")
            base_tm = P.val("tM")
            base_pv = P.val("pV")

            def mk_smm(i):
                def smm(e, hb=hb, sp=i % 2, i=i, qs=qs):
                    e.matmul(ps_s[sp][:, 0:512], ktb[hb][:, (2 * i) * 128:(2 * i + 1) * 128],
                             qtb[hb][:, qs], start=True, stop=True)
                    return e.matmul(ps_s[sp][:, 512:1024],
                                    ktb[hb][:, (2 * i + 1) * 128:(2 * i + 2) * 128],
                                    qtb[hb][:, qs], start=True, stop=True)
                return smm

            if base_se > 0:
                P.wait("t", "sE", base_se - 1)
            P.ein("t", mk_smm(0), "tM")
            if base_se > 0:
                P.wait("t", "sE", base_se)
            P.ein("t", mk_smm(1), "tM")
            for i in range(NPAIR):
                sp = i % 2
                P.wait("s", "tM", base_tm + i + 1)
                if i >= 2:
                    P.wait("s", "pV", base_pv + i - 1)
                P.ein("s", lambda e, sp=sp, h=h: e.activation(
                    pbuf[sp][:], ps_s[sp][:], AF.Exp, bias=bexp_all[:, h:h + 1], scale=1.0), "sE")
                P.wait("t", "sE", base_se + i + 1)
                def pvmm(e, hb=hb, sp=sp, i=i, cpar=cpar, first=(i == 0), last=(i == NPAIR - 1)):
                    e.matmul(ps_c[cpar][:], vtb[hb][:, (2 * i) * 128:(2 * i + 1) * 128],
                             pbuf[sp][:, 0:512], start=first, stop=False)
                    e.matmul(ps_c[cpar][:], vtb[hb][:, (2 * i + 1) * 128:(2 * i + 2) * 128],
                             pbuf[sp][:, 512:1024], start=False, stop=last)
                    e.matmul(ps_l[cpar][:], ones_bf[:], pbuf[sp][:, 0:512], start=first, stop=False)
                    return e.matmul(ps_l[cpar][:], ones_bf[:], pbuf[sp][:, 512:1024],
                                    start=False, stop=last)
                P.ein("t", pvmm, "pV")
                if i + 2 < NPAIR:
                    P.ein("t", mk_smm(i + 2), "tM")
            P.wait("v", "pV", P.val("pV"))
            def levac(e, cpar=cpar):
                e.tensor_copy(l_sb[cpar][:], ps_l[cpar][:])
                e.drain()
                e.reciprocal(linv_sb[cpar][:], l_sb[cpar][:])
                return e.drain()
            P.ein("v", levac, "vE")
            P.wait("g", "vE", P.val("vE"))
            dlw = P.dma("g", linv_d[cpar:cpar + 1, :], linv_sb[cpar][:], sem=flv[cpar])
            P.wait("g", flv[cpar], dlw)
            dbc = P.dma("g", linv_bc[cpar][:], linv_d[cpar:cpar + 1, :].broadcast_to([128, 512]),
                        sem=flb[cpar])
            P.wait("v", flb[cpar], dbc)
            ctx_evac_done[qtglob] = P.ein("v", lambda e, cpar=cpar, h=h, qs=qs: e.tensor_tensor(
                ctxT[h][:, qs], ps_c[cpar][:], linv_bc[cpar][:], ALU.mult), "vE")
            qtglob += 1
            emit_dequant(2)
        P.inc("hD", 1)
        P.em("t", lambda e: e.nop().then_inc(P.sem["hD"], 1))

    if "ctx_d" in debug:
        P.wait("g", "vE", P.val("vE"))
        for h in range(C.NH):
            P.dma("g", ctx_d[h * 128:(h + 1) * 128, :], ctxT[h][:], sem="dF")
    emit_dequant(len(dq_chunks))
    p3ps.close()
    p3.close()
    p5b.close()
    P.barrier()
    if stop_after < 4:
        return nc, P, st

    p4 = ExitStack()
    wo_sb = p4.enter_context(nc.sbuf_tensor("wo_sb", [128, C.DC * C.D], BF16))
    ob_bc = p4.enter_context(nc.sbuf_tensor("ob_bc", [128, C.D], F32))
    arow = [p4.enter_context(nc.sbuf_tensor(f"arow{i}", [128, C.D], F32)) for i in range(2)]
    xo_b = [p4.enter_context(nc.sbuf_tensor(f"xo_b{i}", [128, C.D], F32)) for i in range(2)]
    xn_b = [p4.enter_context(nc.sbuf_tensor(f"xn_b{i}", [128, C.D], F32)) for i in range(2)]
    ln2b = [p4.enter_context(nc.sbuf_tensor(f"ln2b{i}", [128, C.D], BF16)) for i in range(2)]
    lnat1 = p4.enter_context(nc.sbuf_tensor("lnat1", [128, C.D], F32))
    lnat = [lnat1, lnat1]
    s24d = [p4.enter_context(nc.sbuf_tensor(f"s24d{i}", [128, 6 * C.OG], F32)) for i in range(2)]
    mv2d = [p4.enter_context(nc.sbuf_tensor(f"mv2d{i}", [128, 2], F32)) for i in range(2)]
    sd1d = [p4.enter_context(nc.sbuf_tensor(f"sd1d{i}", [128, 1], F32)) for i in range(2)]
    rs1d = [p4.enter_context(nc.sbuf_tensor(f"rs1d{i}", [128, 1], F32)) for i in range(2)]
    nb1d = [p4.enter_context(nc.sbuf_tensor(f"nb1d{i}", [128, 1], F32)) for i in range(2)]
    s24b = [p4.enter_context(nc.sbuf_tensor(f"s24b{i}", [128, 6 * C.OG], F32)) for i in range(2)]
    mv2b = [p4.enter_context(nc.sbuf_tensor(f"mv2b{i}", [128, 2], F32)) for i in range(2)]
    sd1b = [p4.enter_context(nc.sbuf_tensor(f"sd1b{i}", [128, 1], F32)) for i in range(2)]
    rs1b = [p4.enter_context(nc.sbuf_tensor(f"rs1b{i}", [128, 1], F32)) for i in range(2)]
    nb1b = [p4.enter_context(nc.sbuf_tensor(f"nb1b{i}", [128, 1], F32)) for i in range(2)]
    po = [p4.enter_context(nc.psum_tensor(f"po{i}", [128, 512], F32)) for i in range(2)]

    fw4 = P.fam("fw4", 1)
    fxo = P.fam("fxo", 2)
    fxn = P.fam("fxn", 2)
    f2s = P.fam("f2s", 2)
    P.dma_chunked("g", wo_sb, wo_t, fw4[0], 128, C.D, C.DC, max_desc=512)
    dob = P.dma("g", ob_bc[:], ob_r[:].broadcast_to([128, C.D]), sem=fw4[0])
    P.wait("t", fw4[0], dob)
    P.wait("v", fw4[0], dob)

    st1 = {}
    st2 = {}
    st3 = {}

    def p4_s1(tc):
        p = tc % 2
        if tc - 2 in st2:
            P.wait("g", "vE", st2[tc - 2]["vxn"])
        dxo = P.dma("g", xo_b[p][:], x_full[tc * 128:(tc + 1) * 128, :], sem=fxo[p])
        for og in range(C.OG):
            u = tc * C.OG + og
            pp = po[u % 2]
            if u - 2 in p4_ev_done:
                P.wait("t", "vQ", p4_ev_done[u - 2])
            if tc - 2 in st1:
                pass
            def omm(e, tc=tc, og=og, pp=pp):
                last = None
                for dc in range(C.DC):
                    last = e.matmul(pp[:], ctxT[dc][:, tc * 128:(tc + 1) * 128],
                                    wo_sb[:, dc * C.D + og * 512: dc * C.D + (og + 1) * 512],
                                    start=(dc == 0), stop=(dc == C.DC - 1))
                return last
            mmv = P.ein("t", omm, "tM")
            P.wait("v", "tM", mmv)
            p4_ev_done[u] = P.ein("v", lambda e, pp=pp, og=og, p=p: e.tensor_tensor(
                arow[p][:, og * 512:(og + 1) * 512], pp[:],
                ob_bc[:, og * 512:(og + 1) * 512], ALU.add), "vQ")
        def vst1(e, p=p):
            e.drain()
            for j in range(C.OG):
                e.bn_stats(s24b[p][:, j * 6:(j + 1) * 6], arow[p][:, j * 512:(j + 1) * 512])
            e.drain()
            e.bn_aggr(mv2b[p][:], s24b[p][:])
            return e.drain()
        v1 = P.ein("v", vst1, "vS")
        P.wait("s", "vS", v1)
        sq1 = P.ein("s", lambda e, p=p: e.activation(
            sd1b[p][:], mv2b[p][:, 1:2], AF.Sqrt, bias=eps_t[:, 0:1], scale=1.0), "sS")
        st1[tc] = dict(dxo=dxo, sq1=sq1)

    def p4_s2(tc):
        p = tc % 2
        P.wait("v", "sS", st1[tc]["sq1"])
        P.wait("v", fxo[p], st1[tc]["dxo"])
        if tc - 2 in st2:
            P.wait("v", fxn[p], st2[tc - 2]["xst"])
        def vf1(e, p=p):
            e.reciprocal(rs1b[p][:], sd1b[p][:])
            e.drain()
            e.tensor_scalar(nb1b[p][:], mv2b[p][:, 0:1], rs1b[p][:, 0:1], -1.0, ALU.mult, ALU.mult)
            e.drain()
            e.tensor_scalar(lnat[p][:], arow[p][:], rs1b[p][:, 0:1], nb1b[p][:, 0:1],
                            ALU.mult, ALU.add)
            e.drain()
            e.tensor_tensor(xn_b[p][:], xo_b[p][:], lnat[p][:], ALU.add)
            e.drain()
            for j in range(C.OG):
                e.bn_stats(s24d[p][:, j * 6:(j + 1) * 6], xn_b[p][:, j * 512:(j + 1) * 512])
            e.drain()
            e.bn_aggr(mv2d[p][:], s24d[p][:])
            return e.drain()
        vxn = P.ein("v", vf1, "vE")
        P.wait("g", "vE", vxn)
        xst = P.dma("g", xnew_d[tc * 128:(tc + 1) * 128, :], xn_b[p][:], sem=fxn[p])
        P.wait("s", "vE", vxn)
        sq2 = P.ein("s", lambda e, p=p: e.activation(
            sd1d[p][:], mv2d[p][:, 1:2], AF.Sqrt, bias=eps_t[:, 0:1], scale=1.0), "sS")
        st2[tc] = dict(vxn=vxn, xst=xst, sq2=sq2)

    def p4_s3(tc):
        p = tc % 2
        P.wait("v", "sS", st2[tc]["sq2"])
        if tc - 2 in st3:
            P.wait("v", f2s[p], st3[tc - 2])
        def vf2(e, p=p):
            e.reciprocal(rs1d[p][:], sd1d[p][:])
            e.drain()
            e.tensor_scalar(nb1d[p][:], mv2d[p][:, 0:1], rs1d[p][:, 0:1], -1.0, ALU.mult, ALU.mult)
            e.drain()
            e.tensor_scalar(ln2b[p][:], xn_b[p][:], rs1d[p][:, 0:1], nb1d[p][:, 0:1],
                            ALU.mult, ALU.add)
            return e.drain()
        v2 = P.ein("v", vf2, "vS")
        P.wait("g", "vS", v2)
        st3[tc] = P.dma("g", ln2x_d[tc * 128:(tc + 1) * 128, :], ln2b[p][:], sem=f2s[p])

    p4_ev_done = {}
    for tc in range(C.TCO):
        p4_s1(tc)
        if tc >= 1:
            p4_s2(tc - 1)
        if tc >= 2:
            p4_s3(tc - 2)
    p4_s2(C.TCO - 1)
    p4_s3(C.TCO - 2)
    p4_s3(C.TCO - 1)
    p4.close()
    p34.close()
    P.barrier()
    if stop_after < 5:
        return nc, P, st

    P.barrier()
    if stop_after < 6:
        return nc, P, st

    p6 = ExitStack()
    lt2 = p6.enter_context(nc.sbuf_tensor("lt2", [128, C.DC * C.T_OWN], BF16))
    au_sb = p6.enter_context(nc.sbuf_tensor("au_sb", [128, C.DC * C.R], BF16))
    bu_sb = p6.enter_context(nc.sbuf_tensor("bu_sb", [C.R, C.HID], BF16))
    bup_sb = p6.enter_context(nc.sbuf_tensor("bup_sb", [128, C.HC], F32))
    xa_sb = p6.enter_context(nc.sbuf_tensor("xa_sb", [C.R, C.T_OWN], BF16))
    wu_sl = [p6.enter_context(nc.sbuf_tensor(f"wu_sl{i}", [128, C.DC * 512], BF16)) for i in range(2)]
    h2b = [p6.enter_context(nc.sbuf_tensor(f"h2b{i}", [128, 512], BF16)) for i in range(3)]
    pxa = p6.enter_context(nc.psum_tensor("pxa", [C.R, 512], F32))
    pu = [p6.enter_context(nc.psum_tensor(f"pu{i}", [128, 512], F32)) for i in range(2)]

    f6i = P.fam("f6i", 1)
    f6y = P.fam("f6y", 1)
    fwu = P.fam("fwu", 2)
    fh2 = P.fam("fh2", 3)
    dl2v = None
    for dc in range(C.DC):
        for j0 in range(0, C.T_OWN, 1024):
            w = min(1024, C.T_OWN - j0)
            dl2v = P.dmaT(lt2[:, dc * C.T_OWN + j0: dc * C.T_OWN + j0 + w],
                          ln2x_d[j0:j0 + w, dc * 128:(dc + 1) * 128], sem=f6y[0])
    P.dma_chunked("g", au_sb, a_up, f6i[0], 128, C.R, C.DC, max_desc=512)
    P.dma("g", bu_sb[:], b_upl[:], sem=f6i[0])
    dbb = P.dma("g", bup_sb[:], bup_t[:], sem=f6i[0])
    P.wait("t", f6y[0], dl2v)
    P.wait("t", f6i[0], dbb)
    P.wait("s", f6i[0], dbb)

    for t5 in range(C.QT):
        if t5 >= 1:
            P.wait("t", "vQ", P.val("vQ"))
        def xamm(e, t5=t5):
            last = None
            for dc in range(C.DC):
                last = e.matmul(pxa[:], au_sb[:, dc * C.R:(dc + 1) * C.R],
                                lt2[:, dc * C.T_OWN + t5 * 512: dc * C.T_OWN + (t5 + 1) * 512],
                                start=(dc == 0), stop=(dc == C.DC - 1))
            return last
        xv = P.ein("t", xamm, "tM")
        P.wait("v", "tM", xv)
        P.ein("v", lambda e, t5=t5: e.tensor_copy(xa_sb[:, t5 * 512:(t5 + 1) * 512], pxa[:]), "vQ")
    P.wait("t", "vQ", P.val("vQ"))

    p6_ev = {}
    p6_st = {}
    p6_hg_end = {}
    uu = 0
    for hg in range(C.HG):
        wp = hg % 2
        if hg - 2 in p6_hg_end:
            P.wait("y", "tM", p6_hg_end[hg - 2])
        dwv = None
        for dc in range(C.DC):
            dwv = P.dmaT(wu_sl[wp][:, dc * 512:(dc + 1) * 512],
                         wup_deq_d[hg * 512:(hg + 1) * 512, dc * 128:(dc + 1) * 128],
                         sem=fwu[wp])
        P.wait("t", fwu[wp], dwv)
        for hi in range(4):
            hc = hg * 4 + hi
            for t5 in range(C.QT):
                u = uu
                uu += 1
                pp = pu[u % 2]
                if u - 2 in p6_ev:
                    P.wait("t", "sQ", p6_ev[u - 2])
                def umm(e, wp=wp, hi=hi, t5=t5, pp=pp, hc=hc):
                    for dc in range(C.DC):
                        e.matmul(pp[:],
                                 wu_sl[wp][:, dc * 512 + hi * 128: dc * 512 + (hi + 1) * 128],
                                 lt2[:, dc * C.T_OWN + t5 * 512: dc * C.T_OWN + (t5 + 1) * 512],
                                 start=(dc == 0), stop=False)
                    return e.matmul(pp[:], bu_sb[:, hc * 128:(hc + 1) * 128],
                                    xa_sb[:, t5 * 512:(t5 + 1) * 512], start=False, stop=True)
                mmv = P.ein("t", umm, "tM")
                P.wait("s", "tM", mmv)
                if u - 3 in p6_st:
                    P.wait("s", fh2[(u - 3) % 3], p6_st[u - 3])
                hb2 = h2b[u % 3]
                p6_ev[u] = P.ein("s", lambda e, pp=pp, hb2=hb2, hc=hc: e.activation(
                    hb2[:], pp[:], AF.Relu, bias=bup_sb[:, hc:hc + 1], scale=1.0), "sQ")
                P.wait("g", "sQ", p6_ev[u])
                p6_st[u] = P.dma("g", h2_d[hc * 128:(hc + 1) * 128, t5 * 512:(t5 + 1) * 512],
                                 hb2[:], sem=fh2[u % 3])
        p6_hg_end[hg] = P.val("tM")
    p6.close()
    P.barrier()
    if stop_after < 7:
        return nc, P, st

    p7 = ExitStack()
    TH = C.T_OWN // 2
    h2h = p7.enter_context(nc.sbuf_tensor("h2h", [128, C.HC * TH], BF16))
    ad_sb = p7.enter_context(nc.sbuf_tensor("ad_sb", [128, C.HC * C.R], BF16))
    bd_sb = p7.enter_context(nc.sbuf_tensor("bd_sb", [C.R, C.D], BF16))
    bdn_bc = p7.enter_context(nc.sbuf_tensor("bdn_bc", [128, C.D], F32))
    xb_sb = p7.enter_context(nc.sbuf_tensor("xb_sb", [C.R, TH], BF16))
    wd_sl = [p7.enter_context(nc.sbuf_tensor(f"wd_sl{i}", [128, 512], BF16)) for i in range(4)]
    y2b = [p7.enter_context(nc.sbuf_tensor(f"y2b{i}", [128, 512], F32)) for i in range(3)]

    f7i = P.fam("f7i", 1)
    f7h = P.fam("f7h", 1)
    fwd = P.fam("fwd", 4)
    fy2 = P.fam("fy2", 3)
    P.dma_chunked("g", ad_sb, a_dn, f7i[0], 128, C.R, C.HC, max_desc=512)
    P.dma("g", bd_sb[:], b_dnl[:], sem=f7i[0])
    dbc2 = P.dma("g", bdn_bc[:], bdn_r[:].broadcast_to([128, C.D]), sem=f7i[0])
    P.wait("t", f7i[0], dbc2)
    P.wait("v", f7i[0], dbc2)

    NTI = TH // 128
    for th in range(2):
        if th > 0:
            P.wait("g", "tM", P.val("tM"))
            P.wait("y", "tM", P.val("tM"))
            for sy in fy2:
                P.wait("v", sy, P.val(sy))
        dh2 = None
        for hc in range(C.HC):
            dh2 = P.dma("g", h2h[:, hc * TH:(hc + 1) * TH],
                        h2_d[hc * 128:(hc + 1) * 128, th * TH:(th + 1) * TH], sem=f7h[0])
        P.wait("t", f7h[0], dh2)
        pxs = ExitStack()
        pxb = pxs.enter_context(nc.psum_tensor(f"pxb{th}", [C.R, 512], F32))
        for ts0 in range(0, TH, 512):
            w = min(512, TH - ts0)
            if ts0 > 0:
                P.wait("t", "vQ", P.val("vQ"))
            def xbmm(e, ts0=ts0, w=w):
                last = None
                for hc in range(C.HC):
                    last = e.matmul(pxb[:, 0:w], ad_sb[:, hc * C.R:(hc + 1) * C.R],
                                    h2h[:, hc * TH + ts0: hc * TH + ts0 + w],
                                    start=(hc == 0), stop=(hc == C.HC - 1))
                return last
            xv = P.ein("t", xbmm, "tM")
            P.wait("v", "tM", xv)
            P.ein("v", lambda e, ts0=ts0, w=w: e.tensor_copy(
                xb_sb[:, ts0:ts0 + w], pxb[:, 0:w]), "vQ")
        P.wait("t", "vQ", P.val("vQ"))
        pxs.close()
        pds = ExitStack()
        pd = [pds.enter_context(nc.psum_tensor(f"pd{th}_{i}", [128, 512], F32)) for i in range(NTI)]
        wd_mm_end = {}
        y2_st = {}
        for dg5 in range(C.OG):
            for hc in range(C.HC):
                w = dg5 * C.HC + hc
                wp = w % 4
                if w - 4 in wd_mm_end:
                    P.wait("y", "tM", wd_mm_end[w - 4])
                dwv = P.dma("y", wd_sl[wp][:],
                            wdn_deq_d[hc * 128:(hc + 1) * 128, dg5 * 512:(dg5 + 1) * 512],
                            sem=fwd[wp])
                P.wait("t", fwd[wp], dwv)
                def dmm(e, wp=wp, hc=hc, first=(hc == 0)):
                    last = None
                    for ti in range(NTI):
                        last = e.matmul(pd[ti][:],
                                        h2h[:, hc * TH + ti * 128: hc * TH + (ti + 1) * 128],
                                        wd_sl[wp][:], start=first, stop=False)
                    return last
                wd_mm_end[w] = P.ein("t", dmm, "tM")
            def lmm(e, dg5=dg5):
                last = None
                for ti in range(NTI):
                    last = e.matmul(pd[ti][:], xb_sb[:, ti * 128:(ti + 1) * 128],
                                    bd_sb[:, dg5 * 512:(dg5 + 1) * 512], start=False, stop=True)
                return last
            lv = P.ein("t", lmm, "tM")
            P.wait("v", "tM", lv)
            for ti in range(NTI):
                uy = dg5 * NTI + ti
                if uy - 3 in y2_st:
                    P.wait("v", fy2[(uy - 3) % 3], y2_st[uy - 3])
                yb = y2b[uy % 3]
                ev = P.ein("v", lambda e, ti=ti, yb=yb, dg5=dg5: e.tensor_tensor(
                    yb[:], pd[ti][:], bdn_bc[:, dg5 * 512:(dg5 + 1) * 512], ALU.add), "vQ")
                P.wait("g", "vQ", ev)
                y2_st[uy] = P.dma("g", y2_d[th * TH + ti * 128: th * TH + (ti + 1) * 128,
                                            dg5 * 512:(dg5 + 1) * 512], yb[:], sem=fy2[uy % 3])
            P.wait("t", "vQ", P.val("vQ"))
        pds.close()
    p7.close()
    P.barrier()
    if stop_after < 8:
        return nc, P, st

    NP8 = 4
    p8 = ExitStack()
    y2t = [p8.enter_context(nc.sbuf_tensor(f"y2t{i}", [128, C.D], F32)) for i in range(NP8)]
    xnt = [p8.enter_context(nc.sbuf_tensor(f"xnt{i}", [128, C.D], F32)) for i in range(NP8)]
    lnf = [p8.enter_context(nc.sbuf_tensor(f"lnf{i}", [128, C.D], F32)) for i in range(NP8)]
    otb = [p8.enter_context(nc.sbuf_tensor(f"otb{i}", [128, C.D], F32)) for i in range(NP8)]
    s24c = [p8.enter_context(nc.sbuf_tensor(f"s24c{i}", [128, 6 * C.OG], F32)) for i in range(NP8)]
    mv2c = [p8.enter_context(nc.sbuf_tensor(f"mv2c{i}", [128, 2], F32)) for i in range(NP8)]
    sd1c = [p8.enter_context(nc.sbuf_tensor(f"sd1c{i}", [128, 1], F32)) for i in range(NP8)]
    rs1c = [p8.enter_context(nc.sbuf_tensor(f"rs1c{i}", [128, 1], F32)) for i in range(NP8)]
    nb1c = [p8.enter_context(nc.sbuf_tensor(f"nb1c{i}", [128, 1], F32)) for i in range(NP8)]

    f8l = P.fam("f8l", NP8)
    f8o = P.fam("f8o", NP8)
    p8_done = {}
    p8_fv = {}
    p8_stat = {}
    p8_sqrt = {}

    def p8_stage1(tc):
        p = tc % NP8
        if tc >= NP8:
            P.wait("y", f8o[p], p8_done[tc - NP8])
            P.wait("y", "vE", p8_fv[tc - NP8])
        dy2 = P.dma("y", y2t[p][:], y2_d[tc * 128:(tc + 1) * 128, :], sem=f8l[p])
        dxn = P.dma("y", xnt[p][:], xnew_d[tc * 128:(tc + 1) * 128, :], sem=f8l[p])
        P.wait("v", f8l[p], dxn)
        def vstats(e, p=p):
            for j in range(C.OG):
                e.bn_stats(s24c[p][:, j * 6:(j + 1) * 6], y2t[p][:, j * 512:(j + 1) * 512])
            e.drain()
            e.bn_aggr(mv2c[p][:], s24c[p][:])
            return e.drain()
        p8_stat[tc] = P.ein("v", vstats, "vS")
        P.wait("s", "vS", p8_stat[tc])
        p8_sqrt[tc] = P.ein("s", lambda e, p=p: e.activation(
            sd1c[p][:], mv2c[p][:, 1:2], AF.Sqrt, bias=eps_t[:, 0:1], scale=1.0), "sS")

    def p8_stage2(tc):
        p = tc % NP8
        P.wait("v", "sS", p8_sqrt[tc])
        def vfin(e, p=p):
            e.reciprocal(rs1c[p][:], sd1c[p][:])
            e.drain()
            e.tensor_scalar(nb1c[p][:], mv2c[p][:, 0:1], rs1c[p][:, 0:1], -1.0, ALU.mult, ALU.mult)
            e.drain()
            e.tensor_scalar(lnf[p][:], y2t[p][:], rs1c[p][:, 0:1], nb1c[p][:, 0:1], ALU.mult, ALU.add)
            e.drain()
            e.tensor_tensor(otb[p][:], xnt[p][:], lnf[p][:], ALU.add)
            return e.drain()
        p8_fv[tc] = P.ein("v", vfin, "vE")
        P.wait("g", "vE", p8_fv[tc])
        p8_done[tc] = P.dma("g", out[tc * 128:(tc + 1) * 128, :], otb[p][:], sem=f8o[p])

    for tc in range(C.TCO):
        p8_stage1(tc)
        if tc >= 2:
            p8_stage2(tc - 2)
    p8_stage2(C.TCO - 2)
    p8_stage2(C.TCO - 1)
    for s8 in f8o:
        P.wait("g", s8, P.val(s8))
    return nc, P, st


def build(cfg=None, debug=(), stop_after=99):
    cfg = cfg or Cfg()
    nc, P, st = build_nc(cfg, debug=debug, stop_after=stop_after)
    P.emit()
    return nc


def rope_tables(positions, scale):
    import ml_dtypes
    dh = 128
    inv = 1.0 / (10000.0 ** (np.arange(0, dh, 2, dtype=np.float64) / dh))
    freqs = np.outer(positions.astype(np.float64), inv)
    emb = np.concatenate([freqs, freqs], axis=-1)
    cos = np.cos(emb) * scale
    sin = np.sin(emb) * scale
    sin_s = sin.copy()
    sin_s[:, :dh // 2] *= -1.0
    return (cos.astype(np.float32).astype(ml_dtypes.bfloat16),
            sin_s.astype(np.float32).astype(ml_dtypes.bfloat16))


def make_core_inputs(cfg, g, x_batch, W):
    C = cfg
    T = C.T_OWN
    if g == 0:
        xr = x_batch
        pos = np.arange(C.T_FULL)
    else:
        xr = np.concatenate([x_batch[T:], x_batch[:T]], axis=0)
        pos = np.concatenate([np.arange(T, C.T_FULL), np.arange(T)])
    cq, sq = rope_tables(pos[:T], C.SCL)
    ck, sk = rope_tables(pos, 1.0)
    return dict(
        x_full=np.ascontiguousarray(xr),
        wq_t=np.ascontiguousarray(W["q_w"].T), wk_t=np.ascontiguousarray(W["k_w"].T),
        wv_t=np.ascontiguousarray(W["v_w"].T), wo_t=np.ascontiguousarray(W["out_w"].T),
        qb_r=np.ascontiguousarray(W["q_b"].reshape(1, -1)),
        kb_r=np.ascontiguousarray(W["k_b"].reshape(1, -1)),
        vb_r=np.ascontiguousarray(W["v_b"].reshape(1, -1)),
        ob_r=np.ascontiguousarray(W["out_b"].reshape(1, -1)),
        cq_in=np.ascontiguousarray(cq), sq_in=np.ascontiguousarray(sq),
        ck_in=np.ascontiguousarray(ck), sk_in=np.ascontiguousarray(sk),
        wupq=np.ascontiguousarray(W["w_up_q"]), wups=np.ascontiguousarray(W["w_up_scale"]),
        bup_t=np.ascontiguousarray(W["b_up"].reshape(C.HC, 128).T),
        wdnq_t=np.ascontiguousarray(W["w_down_q"].T),
        wdnsc_t=np.ascontiguousarray(W["w_down_scale"].T),
        bdn_r=np.ascontiguousarray(W["b_down"].reshape(1, -1)),
        a_up=np.ascontiguousarray(W["w_up_lora_a"]), b_upl=np.ascontiguousarray(W["w_up_lora_b"]),
        a_dn=np.ascontiguousarray(W["w_down_lora_a"]), b_dnl=np.ascontiguousarray(W["w_down_lora_b"]),
        ident=np.eye(128, dtype=np.float32),
    )


_NC_CACHE = {}


def kernel(**inputs):
    from concourse.bass_utils import run_bass_kernel_spmd
    cfg = Cfg(T_OWN=2048, T_FULL=4096, D=2048, NH=16, HID=8192, R=16)
    W = dict(
        q_w=inputs["L0_attn_q_w"], k_w=inputs["L0_attn_k_w"], v_w=inputs["L0_attn_v_w"],
        out_w=inputs["L0_attn_out_w"], q_b=inputs["L0_attn_q_b"], k_b=inputs["L0_attn_k_b"],
        v_b=inputs["L0_attn_v_b"], out_b=inputs["L0_attn_out_b"],
        w_up_q=inputs["L0_ffn_w_up_q"], w_up_scale=inputs["L0_ffn_w_up_scale"],
        b_up=inputs["L0_ffn_b_up"], w_up_lora_a=inputs["L0_ffn_w_up_lora_a"],
        w_up_lora_b=inputs["L0_ffn_w_up_lora_b"], w_down_q=inputs["L0_ffn_w_down_q"],
        w_down_scale=inputs["L0_ffn_w_down_scale"], b_down=inputs["L0_ffn_b_down"],
        w_down_lora_a=inputs["L0_ffn_w_down_lora_a"], w_down_lora_b=inputs["L0_ffn_w_down_lora_b"],
    )
    W = {k: np.asarray(v) for k, v in W.items()}
    x = np.asarray(inputs["L0_attn_x"])
    B = x.shape[0]

    if "nc" not in _NC_CACHE:
        _NC_CACHE["nc"] = build(cfg)
    nc = _NC_CACHE["nc"]

    shared = None
    in_maps = []
    for c in range(2 * B):
        b, g = c // 2, c % 2
        m = make_core_inputs(cfg, g, x[b], W)
        if shared is None:
            shared = {k: m[k] for k in ("wq_t", "wk_t", "wv_t", "wo_t", "qb_r", "kb_r",
                                        "vb_r", "ob_r", "wupq", "wups", "bup_t", "wdnq_t",
                                        "wdnsc_t", "bdn_r", "a_up", "b_upl", "a_dn", "b_dnl",
                                        "ident")}
        else:
            m.update(shared)
        in_maps.append(m)

    res = run_bass_kernel_spmd(nc, in_maps, core_ids=list(range(2 * B)))
    out = np.empty((B, cfg.T_FULL, cfg.D), dtype=np.float32)
    for c in range(2 * B):
        b, g = c // 2, c % 2
        out[b, g * cfg.T_OWN:(g + 1) * cfg.T_OWN] = np.asarray(res.results[c]["out"])
    return out


# version 20
# speedup vs baseline: 1.0542x; 1.0542x over previous
import math
import sys
from contextlib import ExitStack

import numpy as np

try:
    import concourse.bass as bass
    import concourse.mybir as mybir
except ImportError:
    sys.path.insert(0, "/opt/trn_rl_repo")
    import concourse.bass as bass
    import concourse.mybir as mybir

F32 = mybir.dt.float32
BF16 = mybir.dt.bfloat16
I32 = mybir.dt.int32
ALU = mybir.AluOpType
AF = mybir.ActivationFunctionType
AX = mybir.AxisListType.X

LN_EPS = 1e-5


class Cfg:
    def __init__(self, T_OWN=2048, T_FULL=4096, D=2048, NH=16, HID=8192, R=16):
        self.T_OWN, self.T_FULL, self.D, self.NH, self.HID, self.R = T_OWN, T_FULL, D, NH, HID, R
        self.DH = 128
        assert self.NH * self.DH == self.D
        self.DC = D // 128
        self.OG = D // 512
        self.HC = HID // 128
        self.HG = HID // 512
        self.TCO = T_OWN // 128
        self.TCF = T_FULL // 128
        self.QT = T_OWN // 512
        self.KC = T_FULL // 128
        self.SCL = self.DH ** -0.5


class Prog:
    ENGS = ("g", "v", "t", "s", "y")

    def __init__(self, nc, stack):
        self.nc = nc
        self.ops = {e: [] for e in self.ENGS}
        self.cnt = {}
        self.sem = {}
        self.stack = stack

    def mksem(self, *names):
        for n in names:
            self.sem[n] = self.stack.enter_context(self.nc.semaphore(n))
            self.cnt[n] = 0

    def fam(self, base, nbuf):
        names = [f"{base}{i}" for i in range(nbuf)]
        self.mksem(*names)
        return names

    def em(self, eng, fn):
        self.ops[eng].append(fn)

    def wait(self, eng, sem, value):
        if value is None or value <= 0:
            return
        self.em(eng, lambda e, s=self.sem[sem], v=value: e.wait_ge(s, v))

    def inc(self, sem, n):
        self.cnt[sem] += n
        return self.cnt[sem]

    def val(self, sem):
        return self.cnt[sem]

    def dma(self, eng, outap, inap, sem="dg", **kw):
        v = self.inc(sem, 16)
        so = self.sem[sem]
        self.em(eng, lambda e, o=outap, i=inap, s=so, kw=kw:
                e.dma_start(out=o, in_=i, **kw).then_inc(s, 16))
        return v

    def dma_chunked(self, eng, dst, srcdram, sem, p, inner, nchunks, max_desc=1024):
        per = max(1, max_desc // 128)
        v = None
        for c0 in range(0, nchunks, per):
            c1 = min(nchunks, c0 + per)
            v = self.dma(eng,
                         dst[:, c0 * inner:c1 * inner].rearrange("p (c d) -> p c d", d=inner),
                         srcdram[c0 * 128:c1 * 128, :].rearrange("(c p) d -> p c d", p=128),
                         sem=sem)
        return v

    def dmaT(self, outap, inap, sem="dy"):
        v = self.inc(sem, 16)
        so = self.sem[sem]
        self.em("y", lambda e, o=outap, i=inap, s=so:
                e.dma_start_transpose(out=o, in_=i).then_inc(s, 16))
        return v

    def ein(self, eng, fn, sem, n=1):
        v = self.inc(sem, n)
        so = self.sem[sem]
        self.em(eng, lambda e, f=fn, s=so, k=n: f(e).then_inc(s, k))
        return v

    def barrier(self):
        vals = {k: v for k, v in self.cnt.items() if v > 0}
        for eng in self.ENGS:
            for k, v in vals.items():
                self.wait(eng, k, v)

    def emit(self):
        with self.nc.Block() as block:
            @block.gpsimd
            def _(e):
                for fn in self.ops["g"]:
                    fn(e)

            @block.vector
            def _(e):
                for fn in self.ops["v"]:
                    fn(e)

            @block.tensor
            def _(e):
                for fn in self.ops["t"]:
                    fn(e)

            @block.scalar
            def _(e):
                for fn in self.ops["s"]:
                    fn(e)

            @block.sync
            def _(e):
                for fn in self.ops["y"]:
                    fn(e)


def build_nc(cfg: Cfg, debug=(), stop_after=99):
    nc = bass.Bass("TRN2", target_bir_lowering=False)
    C = cfg
    dk = lambda name: "ExternalOutput" if name in debug else "Internal"

    x_full = nc.dram_tensor("x_full", [C.T_FULL, C.D], F32, kind="ExternalInput")
    wq_t = nc.dram_tensor("wq_t", [C.D, C.D], F32, kind="ExternalInput")
    wk_t = nc.dram_tensor("wk_t", [C.D, C.D], F32, kind="ExternalInput")
    wv_t = nc.dram_tensor("wv_t", [C.D, C.D], F32, kind="ExternalInput")
    wo_t = nc.dram_tensor("wo_t", [C.D, C.D], F32, kind="ExternalInput")
    qb_r = nc.dram_tensor("qb_r", [1, C.D], F32, kind="ExternalInput")
    kb_r = nc.dram_tensor("kb_r", [1, C.D], F32, kind="ExternalInput")
    vb_r = nc.dram_tensor("vb_r", [1, C.D], F32, kind="ExternalInput")
    ob_r = nc.dram_tensor("ob_r", [1, C.D], F32, kind="ExternalInput")
    cq_in = nc.dram_tensor("cq_in", [C.T_OWN, 128], BF16, kind="ExternalInput")
    sq_in = nc.dram_tensor("sq_in", [C.T_OWN, 128], BF16, kind="ExternalInput")
    ck_in = nc.dram_tensor("ck_in", [C.T_FULL, 128], BF16, kind="ExternalInput")
    sk_in = nc.dram_tensor("sk_in", [C.T_FULL, 128], BF16, kind="ExternalInput")
    wupq = nc.dram_tensor("wupq", [C.HID, C.D], I32, kind="ExternalInput")
    wups = nc.dram_tensor("wups", [C.HID, C.D // 64], F32, kind="ExternalInput")
    bup_t = nc.dram_tensor("bup_t", [128, C.HC], F32, kind="ExternalInput")
    wdnq_t = nc.dram_tensor("wdnq_t", [C.HID, C.D], I32, kind="ExternalInput")
    wdnsc_t = nc.dram_tensor("wdnsc_t", [C.HID // 64, C.D], F32, kind="ExternalInput")
    bdn_r = nc.dram_tensor("bdn_r", [1, C.D], F32, kind="ExternalInput")
    a_up = nc.dram_tensor("a_up", [C.D, C.R], F32, kind="ExternalInput")
    b_upl = nc.dram_tensor("b_upl", [C.R, C.HID], F32, kind="ExternalInput")
    a_dn = nc.dram_tensor("a_dn", [C.HID, C.R], F32, kind="ExternalInput")
    b_dnl = nc.dram_tensor("b_dnl", [C.R, C.D], F32, kind="ExternalInput")
    ident = nc.dram_tensor("ident", [128, 128], F32, kind="ExternalInput")

    out = nc.dram_tensor("out", [C.T_OWN, C.D], F32, kind="ExternalOutput")

    ln1x_d = nc.dram_tensor("ln1x_d", [C.T_FULL, C.D], BF16, kind=dk("ln1x_d"))
    q_d = nc.dram_tensor("q_d", [C.T_OWN, C.D], BF16, kind=dk("q_d"))
    k_d = nc.dram_tensor("k_d", [C.T_FULL, C.D], BF16, kind=dk("k_d"))
    v_d = nc.dram_tensor("v_d", [C.T_FULL, C.D], BF16, kind=dk("v_d"))
    cneg_d = nc.dram_tensor("cneg_d", [C.NH, 1], F32, kind=dk("cneg_d"))
    ctx_d = nc.dram_tensor("ctx_d", [C.NH * 128, C.T_OWN], BF16, kind=dk("ctx_d"))
    xnew_d = nc.dram_tensor("xnew_d", [C.T_OWN, C.D], F32, kind=dk("xnew_d"))
    ln2x_d = nc.dram_tensor("ln2x_d", [C.T_OWN, C.D], BF16, kind=dk("ln2x_d"))
    wup_deq_d = nc.dram_tensor("wup_deq_d", [C.HID, C.D], BF16, kind=dk("wup_deq_d"))
    wdn_deq_d = nc.dram_tensor("wdn_deq_d", [C.HID, C.D], BF16, kind=dk("wdn_deq_d"))
    h2_d = nc.dram_tensor("h2_d", [C.HID, C.T_OWN], BF16, kind=dk("h2_d"))
    y2_d = nc.dram_tensor("y2_d", [C.T_OWN, C.D], F32, kind=dk("y2_d"))
    linv_d = nc.dram_tensor("linv_d", [4, 512], F32, kind="Internal")
    nred_d = nc.dram_tensor("nred_d", [2 * C.NH, 1], F32, kind="Internal")

    st = ExitStack()
    P = Prog(nc, st)
    P.mksem("dg", "dy", "dO", "dX", "vS", "sS", "tM", "sE", "vE", "pV", "hD", "vQ", "vR", "sQ", "dF")

    eps_t = st.enter_context(nc.sbuf_tensor("eps_t", [128, 1], F32))
    zero_t = st.enter_context(nc.sbuf_tensor("zero_t", [128, 1], F32))
    ones_bf = st.enter_context(nc.sbuf_tensor("ones_bf", [128, 1], BF16))
    ident_sb = st.enter_context(nc.sbuf_tensor("ident_sb", [128, 128], F32))
    bexp_all = st.enter_context(nc.sbuf_tensor("bexp_all", [128, C.NH], F32))

    P.em("v", lambda e: e.memset(eps_t[:], LN_EPS))
    P.em("v", lambda e: e.memset(zero_t[:], 0.0))
    P.em("v", lambda e: e.memset(ones_bf[:], 1.0))
    P.dma("g", ident_sb[:], ident[:])

    def ln_chain(xap, outap, s24, m2, s1, r1, n1):
        def vstats(e):
            for j in range(C.OG):
                e.bn_stats(s24[:, j * 6:(j + 1) * 6], xap[:, j * 512:(j + 1) * 512])
            e.drain()
            e.bn_aggr(m2[:], s24[:])
            return e.drain()
        P.ein("v", vstats, "vS")
        P.wait("s", "vS", P.val("vS"))
        P.ein("s", lambda e: e.activation(s1[:], m2[:, 1:2], AF.Sqrt,
                                          bias=eps_t[:, 0:1], scale=1.0), "sS")
        P.wait("v", "sS", P.val("sS"))
        def vfin(e):
            e.reciprocal(r1[:], s1[:])
            e.drain()
            e.tensor_scalar(n1[:], m2[:, 0:1], r1[:, 0:1], -1.0, ALU.mult, ALU.mult)
            e.drain()
            e.tensor_scalar(outap, xap, r1[:, 0:1], n1[:, 0:1], ALU.mult, ALU.add)
            return e.drain()
        return P.ein("v", vfin, "vS")

    NP1 = 4
    p1 = ExitStack()
    xb = [p1.enter_context(nc.sbuf_tensor(f"xb{i}", [128, C.D], F32)) for i in range(NP1)]
    lnb = [p1.enter_context(nc.sbuf_tensor(f"lnb{i}", [128, C.D], BF16)) for i in range(NP1)]
    s24 = [p1.enter_context(nc.sbuf_tensor(f"s24_{i}", [128, 6 * C.OG], F32)) for i in range(NP1)]
    mv2 = [p1.enter_context(nc.sbuf_tensor(f"mv2_{i}", [128, 2], F32)) for i in range(NP1)]
    sd1 = [p1.enter_context(nc.sbuf_tensor(f"sd1_{i}", [128, 1], F32)) for i in range(NP1)]
    rs1 = [p1.enter_context(nc.sbuf_tensor(f"rs1_{i}", [128, 1], F32)) for i in range(NP1)]
    nb1 = [p1.enter_context(nc.sbuf_tensor(f"nb1_{i}", [128, 1], F32)) for i in range(NP1)]

    fx = P.fam("fx", NP1)
    fln = P.fam("fln", NP1)
    apply_done = {}
    store_done = {}
    stat_done = {}
    sqrt_done = {}

    def p1_stage1(tc):
        p = tc % NP1
        if tc >= NP1:
            P.wait("y", "vS", apply_done[tc - NP1])
        dv = P.dma("y", xb[p][:], x_full[tc * 128:(tc + 1) * 128, :], sem=fx[p])
        P.wait("v", fx[p], dv)
        def vstats(e, p=p):
            for j in range(C.OG):
                e.bn_stats(s24[p][:, j * 6:(j + 1) * 6], xb[p][:, j * 512:(j + 1) * 512])
            e.drain()
            e.bn_aggr(mv2[p][:], s24[p][:])
            return e.drain()
        stat_done[tc] = P.ein("v", vstats, "vS")
        P.wait("s", "vS", stat_done[tc])
        sqrt_done[tc] = P.ein("s", lambda e, p=p: e.activation(
            sd1[p][:], mv2[p][:, 1:2], AF.Sqrt, bias=eps_t[:, 0:1], scale=1.0), "sS")

    def p1_stage2(tc):
        p = tc % NP1
        P.wait("v", "sS", sqrt_done[tc])
        if tc >= NP1:
            P.wait("v", fln[p], store_done[tc - NP1])
        def vfin(e, p=p):
            e.reciprocal(rs1[p][:], sd1[p][:])
            e.drain()
            e.tensor_scalar(nb1[p][:], mv2[p][:, 0:1], rs1[p][:, 0:1], -1.0, ALU.mult, ALU.mult)
            e.drain()
            e.tensor_scalar(lnb[p][:], xb[p][:], rs1[p][:, 0:1], nb1[p][:, 0:1], ALU.mult, ALU.add)
            return e.drain()
        apply_done[tc] = P.ein("v", vfin, "vS")
        P.wait("s", "vS", apply_done[tc])
        store_done[tc] = P.dma("s", ln1x_d[tc * 128:(tc + 1) * 128, :], lnb[p][:], sem=fln[p])

    for tc in range(C.TCF):
        p1_stage1(tc)
        if tc >= 2:
            p1_stage2(tc - 2)
    p1_stage2(C.TCF - 2)
    p1_stage2(C.TCF - 1)
    if stop_after < 2:
        p1.close()
        P.barrier()
        return nc, P, st

    p2 = ExitStack()
    wfull = p2.enter_context(nc.sbuf_tensor("wfull", [128, C.DC * C.D], BF16))
    ltb = [p2.enter_context(nc.sbuf_tensor(f"ltb{i}", [128, C.DC * 512], BF16)) for i in range(2)]
    bias_bc = p2.enter_context(nc.sbuf_tensor("bias_bc", [128, C.D], F32))
    rbuf = [p2.enter_context(nc.sbuf_tensor(f"rbuf{i}", [128, 512], BF16)) for i in range(3)]
    tmpa = p2.enter_context(nc.sbuf_tensor("tmpa", [128, 512], BF16))
    tmpb = p2.enter_context(nc.sbuf_tensor("tmpb", [128, 512], BF16))
    obuf = [p2.enter_context(nc.sbuf_tensor(f"obuf{i}", [128, 512], BF16)) for i in range(3)]
    sqb = [p2.enter_context(nc.sbuf_tensor(f"sqb{i}", [128, 512], F32)) for i in range(2)]
    ckt = p2.enter_context(nc.sbuf_tensor("ckt", [128, C.TCF * 128], BF16))
    skt = p2.enter_context(nc.sbuf_tensor("skt", [128, C.TCF * 128], BF16))
    cqt = p2.enter_context(nc.sbuf_tensor("cqt", [128, C.TCO * 128], BF16))
    sqt = p2.enter_context(nc.sbuf_tensor("sqt", [128, C.TCO * 128], BF16))
    nsq_q = p2.enter_context(nc.sbuf_tensor("nsq_q", [128, C.NH * C.TCO], F32))
    nsq_k = p2.enter_context(nc.sbuf_tensor("nsq_k", [128, C.NH * C.TCF], F32))
    pqk = [p2.enter_context(nc.psum_tensor(f"pqk{i}", [128, 512], F32)) for i in range(4)]

    dtab = None
    for (dst, src) in ((ckt, ck_in), (skt, sk_in), (cqt, cq_in), (sqt, sq_in)):
        dtab = P.dma_chunked("g", dst, src, "dg", 128, 128, src.shape[0] // 128, max_desc=512)
    P.em("v", lambda e: e.memset(nsq_q[:], 0.0))
    P.em("v", lambda e: e.memset(nsq_k[:], 0.0))
    P.wait("v", "dg", dtab)

    PASSES = [
        ("k", wk_t, kb_r, k_d, C.TCF, True, ckt, skt, nsq_k, C.TCF),
        ("v", wv_t, vb_r, v_d, C.TCF, False, None, None, None, 0),
        ("q", wq_t, qb_r, q_d, C.TCO, True, cqt, sqt, nsq_q, C.TCO),
    ]

    uglob = 0
    evac_done = {}
    rope_done = {}
    sq_done = {}
    nsq_done = {}
    out_done = {}
    t5_mm_end = {}
    hpg = 512 // C.DH

    def emit_nsq(u0, og0, tc0, nsq0, NSQT0):
        P.wait("v", "sQ", sq_done[u0])
        def nr(e, og=og0, tc=tc0, nsq=nsq0, NSQT=NSQT0, sb=sqb[u0 % 2]):
            o3 = nsq[:].rearrange("p (h t) -> p h t", t=NSQT)
            e.reduce_sum(o3[:, og * hpg:(og + 1) * hpg, tc],
                         sb[:].rearrange("p (h d) -> p h d", d=128), AX)
            return e.drain()
        nsq_done[u0] = P.ein("v", nr, "vE")

    pending_nsq = {}
    last_nsq_flush = [0]
    flt = P.fam("flt", 2)
    fw2 = P.fam("fw2", 1)
    fqv = P.fam("fqv", 3)
    for pi, (pname, wsrc, bsrc, dsth, TC, do_rope, ctab, stab, nsq, NSQT) in enumerate(PASSES):
        if pi > 0:
            P.wait("g", "tM", P.val("tM"))
            P.wait("g", "vQ", P.val("vQ"))
        if do_rope and last_nsq_flush[0] > 0:
            P.wait("s", "vE", last_nsq_flush[0])
        P.dma_chunked("g", wfull, wsrc, fw2[0], 128, C.D, C.DC, max_desc=512)
        db = P.dma("g", bias_bc[:], bsrc[:].broadcast_to([128, C.D]), sem=fw2[0])
        P.wait("t", fw2[0], db)
        P.wait("v", fw2[0], db)
        n512 = TC // 4
        pass_t5_base = P.val("tM")
        for t5 in range(n512):
            lp = t5 % 2
            if (pi, t5 - 2) in t5_mm_end:
                P.wait("y", "tM", t5_mm_end[(pi, t5 - 2)])
            elif t5 < 2 and pi > 0:
                P.wait("y", "tM", pass_t5_base)
            for j in range(4):
                tile = t5 * 4 + j
                P.wait("y", fln[tile % NP1], store_done[tile])
            dtv = None
            for dc in range(C.DC):
                dtv = P.dmaT(ltb[lp][:, dc * 512:(dc + 1) * 512],
                             ln1x_d[t5 * 512:(t5 + 1) * 512, dc * 128:(dc + 1) * 128],
                             sem=flt[lp])
            P.wait("t", flt[lp], dtv)
            for t1 in range(4):
                tc = t5 * 4 + t1
                for og in range(C.OG):
                    u = uglob
                    uglob += 1
                    pp = pqk[u % 4]
                    bufi = u % 3
                    if u - 4 in evac_done:
                        P.wait("t", "vQ", evac_done[u - 4])
                    def mmg(e, lp=lp, t1=t1, og=og, pp=pp):
                        last = None
                        for dc in range(C.DC):
                            last = e.matmul(
                                pp[:],
                                ltb[lp][:, dc * 512 + t1 * 128: dc * 512 + (t1 + 1) * 128],
                                wfull[:, dc * C.D + og * 512: dc * C.D + (og + 1) * 512],
                                start=(dc == 0), stop=(dc == C.DC - 1))
                        return last
                    mmv = P.ein("t", mmg, "tM")
                    P.wait("v", "tM", mmv)
                    rb = rbuf[bufi] if do_rope else obuf[bufi]
                    if u - 3 in out_done:
                        P.wait("v", fqv[(u - 3) % 3], out_done[u - 3])
                    if not do_rope and u - 3 in sq_done:
                        P.wait("v", "sQ", sq_done[u - 3])
                    evac_done[u] = P.ein("v", lambda e, pp=pp, rb=rb, og=og: e.tensor_tensor(
                        rb[:], pp[:], bias_bc[:, og * 512:(og + 1) * 512], ALU.add), "vQ")
                    if do_rope:
                        ob = obuf[bufi]
                        if u - 3 in sq_done:
                            P.wait("v", "sQ", sq_done[u - 3])
                        def ropec(e, rb=rb, ob=ob, tc=tc, ctab=ctab, stab=stab):
                            e.drain()
                            ctile = ctab[:, tc * 128:(tc + 1) * 128].rearrange(
                                "p (r d) -> p r d", r=1).broadcast_to([128, hpg, 128])
                            s0 = stab[:, tc * 128:tc * 128 + 64].rearrange(
                                "p (r d) -> p r d", r=1).broadcast_to([128, hpg, 64])
                            s1 = stab[:, tc * 128 + 64:(tc + 1) * 128].rearrange(
                                "p (r d) -> p r d", r=1).broadcast_to([128, hpg, 64])
                            r3 = rb[:].rearrange("p (c d) -> p c d", d=128)
                            r4 = rb[:].rearrange("p (c two d) -> p c two d", two=2, d=64)
                            ta3 = tmpa[:].rearrange("p (c d) -> p c d", d=128)
                            tb4 = tmpb[:].rearrange("p (c two d) -> p c two d", two=2, d=64)
                            e.tensor_tensor(ta3, r3, ctile, ALU.mult)
                            e.tensor_tensor(tb4[:, :, 0, :], r4[:, :, 1, :], s0, ALU.mult)
                            e.tensor_tensor(tb4[:, :, 1, :], r4[:, :, 0, :], s1, ALU.mult)
                            e.drain()
                            e.tensor_tensor(ob[:], tmpa[:], tmpb[:], ALU.add)
                            return e.drain()
                        rope_done[u] = P.ein("v", ropec, "vR")
                        if u - 2 in pending_nsq:
                            emit_nsq(*pending_nsq.pop(u - 2))
                            P.wait("s", "vE", nsq_done[u - 2])
                        P.wait("s", "vR", rope_done[u])
                        sq_done[u] = P.ein("s", lambda e, ob=ob, sb=sqb[u % 2]:
                                           e.activation(sb[:], ob[:], AF.Square), "sQ")
                        pending_nsq[u] = (u, og, tc, nsq, NSQT)
                        P.wait("g", "vR", rope_done[u])
                    else:
                        ob = rb
                        P.wait("g", "vQ", evac_done[u])
                    out_done[u] = P.dma("g", dsth[tc * 128:(tc + 1) * 128,
                                                 og * 512:(og + 1) * 512], ob[:], sem=fqv[u % 3])

            t5_mm_end[(pi, t5)] = P.val("tM")
        for u0 in sorted(pending_nsq):
            emit_nsq(*pending_nsq[u0])
        pending_nsq.clear()
        last_nsq_flush[0] = P.val("vE")

    fin = ExitStack()
    nmax = fin.enter_context(nc.sbuf_tensor("nmax", [128, 2 * C.NH], F32))
    ntp = fin.enter_context(nc.psum_tensor("ntp", [2 * C.NH, 128], F32))
    nfin = fin.enter_context(nc.sbuf_tensor("nfin", [2 * C.NH, 128], F32))
    nred = fin.enter_context(nc.sbuf_tensor("nred", [2 * C.NH, 1], F32))
    nsqt2 = fin.enter_context(nc.sbuf_tensor("nsqt2", [C.NH, 2], F32))
    cneg = fin.enter_context(nc.sbuf_tensor("cneg", [C.NH, 1], F32))

    def nmaxc(e):
        e.reduce_max(nmax[:, 0:C.NH], nsq_q[:].rearrange("p (h t) -> p h t", t=C.TCO), AX)
        e.reduce_max(nmax[:, C.NH:2 * C.NH], nsq_k[:].rearrange("p (h t) -> p h t", t=C.TCF), AX)
        return e.drain()
    P.ein("v", nmaxc, "vE")
    P.wait("t", "vE", P.val("vE"))
    P.ein("t", lambda e: e.transpose(ntp[:], nmax[:], ident_sb[:]), "tM")
    P.wait("v", "tM", P.val("tM"))
    def cnc(e):
        e.tensor_copy(nfin[:], ntp[:])
        e.drain()
        e.reduce_max(nred[:], nfin[:], AX)
        return e.drain()
    P.ein("v", cnc, "vE")
    fcn = P.fam("fcn", 1)
    P.wait("g", "vE", P.val("vE"))
    dnr = P.dma("g", nred_d[:, :], nred[:], sem=fcn[0])
    P.wait("g", fcn[0], dnr)
    P.dma("g", nsqt2[:, 0:1], nred_d[0:C.NH, :], sem=fcn[0])
    dnl = P.dma("g", nsqt2[:, 1:2], nred_d[C.NH:2 * C.NH, :], sem=fcn[0])
    P.wait("v", fcn[0], dnl)
    def cn2a(e):
        e.tensor_tensor(cneg[:], nsqt2[:, 0:1], nsqt2[:, 1:2], ALU.mult)
        return e.drain()
    P.ein("v", cn2a, "vE")
    P.wait("s", "vE", P.val("vE"))
    P.ein("s", lambda e: e.activation(cneg[:], cneg[:], AF.Sqrt,
                                      bias=zero_t[0:C.NH, 0:1], scale=1.0), "sS")
    P.wait("v", "sS", P.val("sS"))
    def cn2b(e):
        e.tensor_scalar(cneg[:], cneg[:], -1.01, -0.01, ALU.mult, ALU.add)
        return e.drain()
    P.ein("v", cn2b, "vE")
    P.wait("g", "vE", P.val("vE"))
    dcw = P.dma("g", cneg_d[:, :], cneg[:], sem=fcn[0])
    P.wait("g", fcn[0], dcw)
    for h in range(C.NH):
        P.dma("g", bexp_all[:, h:h + 1], cneg_d[h:h + 1, 0:1].broadcast_to([128, 1]), sem=fcn[0])
    fin.close()
    p2.close()
    p1.close()
    P.barrier()
    if stop_after < 3:
        return nc, P, st

    p34 = ExitStack()
    ctxT = [p34.enter_context(nc.sbuf_tensor(f"ctxT{h}", [128, C.T_OWN], BF16))
            for h in range(C.NH)]

    p5b = ExitStack()
    iqb5 = [p5b.enter_context(nc.sbuf_tensor(f"iqb5_{i}", [128, C.D], I32)) for i in range(2)]
    scb5 = [p5b.enter_context(nc.sbuf_tensor(f"scb5_{i}", [128, C.D // 64], F32)) for i in range(2)]
    sbc5 = [p5b.enter_context(nc.sbuf_tensor(f"sbc5_{i}", [128, C.D], F32)) for i in range(2)]
    dqb5 = [p5b.enter_context(nc.sbuf_tensor(f"dqb5_{i}", [128, C.D], BF16)) for i in range(2)]
    P.mksem("vD")
    fdl = P.fam("fdl", 2)
    fds = P.fam("fds", 2)
    dq_chunks = [("up", rc) for rc in range(C.HC)] + [("dn", hc) for hc in range(C.HC)]
    dq_state = {"i": 0, "store": {}, "deq": {}}

    def emit_dequant(n):
        for _ in range(n):
            i = dq_state["i"]
            if i >= len(dq_chunks):
                return
            dq_state["i"] = i + 1
            kind, rc = dq_chunks[i]
            p = i % 2
            if i - 2 in dq_state["store"]:
                P.wait("g", fds[p], dq_state["store"][i - 2])
                P.wait("v", fds[p], dq_state["store"][i - 2])
            if kind == "up":
                P.dma("g", iqb5[p][:], wupq[rc * 128:(rc + 1) * 128, :], sem=fdl[p])
                ds = P.dma("g", scb5[p][:], wups[rc * 128:(rc + 1) * 128, :], sem=fdl[p])
                P.wait("v", fdl[p], ds)
                def dqc(e, p=p):
                    q3 = iqb5[p][:].rearrange("p (c d) -> p c d", d=64)
                    o3 = dqb5[p][:].rearrange("p (c d) -> p c d", d=64)
                    s3 = scb5[p][:].rearrange("p (c d) -> p c d", d=1).broadcast_to(
                        [128, C.D // 64, 64])
                    e.scalar_tensor_tensor(o3, q3, -7.5, s3, ALU.add, ALU.mult)
                    return e.drain()
                dq_state["deq"][i] = P.ein("v", dqc, "vD")
                P.wait("y", "vD", dq_state["deq"][i])
                dq_state["store"][i] = P.dma(
                    "y", wup_deq_d[rc * 128:(rc + 1) * 128, :], dqb5[p][:], sem=fds[p])
            else:
                P.dma("g", iqb5[p][:], wdnq_t[rc * 128:(rc + 1) * 128, :], sem=fdl[p])
                P.dma("g", sbc5[p][0:64, :],
                      wdnsc_t[2 * rc:2 * rc + 1, :].broadcast_to([64, C.D]), sem=fdl[p])
                ds = P.dma("g", sbc5[p][64:128, :],
                           wdnsc_t[2 * rc + 1:2 * rc + 2, :].broadcast_to([64, C.D]), sem=fdl[p])
                P.wait("v", fdl[p], ds)
                def dqc2(e, p=p):
                    e.scalar_tensor_tensor(dqb5[p][:], iqb5[p][:], -7.5, sbc5[p][:],
                                           ALU.add, ALU.mult)
                    return e.drain()
                dq_state["deq"][i] = P.ein("v", dqc2, "vD")
                P.wait("y", "vD", dq_state["deq"][i])
                dq_state["store"][i] = P.dma(
                    "y", wdn_deq_d[rc * 128:(rc + 1) * 128, :], dqb5[p][:], sem=fds[p])
    p3 = ExitStack()
    ktb = [p3.enter_context(nc.sbuf_tensor(f"ktb{i}", [128, C.T_FULL], BF16)) for i in range(2)]
    qtb = [p3.enter_context(nc.sbuf_tensor(f"qtb{i}", [128, C.T_OWN], BF16)) for i in range(2)]
    vtb = [p3.enter_context(nc.sbuf_tensor(f"vtb{i}", [128, C.T_FULL], BF16)) for i in range(2)]
    pbuf = [p3.enter_context(nc.sbuf_tensor(f"pbuf{i}", [128, 1024], BF16)) for i in range(2)]
    l_sb = [p3.enter_context(nc.sbuf_tensor(f"l_sb{i}", [1, 512], F32)) for i in range(2)]
    linv_sb = [p3.enter_context(nc.sbuf_tensor(f"linv_sb{i}", [1, 512], F32)) for i in range(2)]
    linv_bc = [p3.enter_context(nc.sbuf_tensor(f"linv_bc{i}", [128, 512], F32)) for i in range(2)]
    p3ps = ExitStack()
    ps_s = [p3ps.enter_context(nc.psum_tensor(f"ps_s{i}", [128, 1024], F32)) for i in range(2)]
    ps_c = [p3ps.enter_context(nc.psum_tensor(f"ps_c{i}", [128, 512], F32)) for i in range(2)]
    ps_l = [p3ps.enter_context(nc.psum_tensor(f"ps_l{i}", [64, 512], F32)) for i in range(2)]
    l_tmp = p3.enter_context(nc.sbuf_tensor("l_tmp", [1, 512], F32))

    NPAIR = C.KC // 2
    fkq = P.fam("fkq", 2)
    fv3 = P.fam("fv3", 2)
    flv = P.fam("flv", 2)
    flb = P.fam("flb", 2)

    def head_loads(h, hb):
        hs = slice(h * 128, (h + 1) * 128)
        dyv = None
        for j0 in range(0, C.T_FULL, 1024):
            w = min(1024, C.T_FULL - j0)
            dyv = P.dmaT(ktb[hb][:, j0:j0 + w], k_d[j0:j0 + w, hs], sem=fkq[hb])
        for j0 in range(0, C.T_OWN, 1024):
            w = min(1024, C.T_OWN - j0)
            dyv = P.dmaT(qtb[hb][:, j0:j0 + w], q_d[j0:j0 + w, hs], sem=fkq[hb])
        dgv = P.dma_chunked("g", vtb[hb], v_d[:, hs], fv3[hb], 128, 128, C.KC, max_desc=512)
        return dyv, dgv

    qtglob = 0
    ctx_evac_done = {}
    loads = [head_loads(0, 0)]
    for h in range(C.NH):
        hb = h % 2
        dyv, dgv = loads[h]
        P.wait("t", fkq[hb], dyv)
        P.wait("t", fv3[hb], dgv)
        if h + 1 < C.NH:
            if h >= 1:
                P.wait("y", "hD", h)
                P.wait("g", "hD", h)
            loads.append(head_loads(h + 1, (h + 1) % 2))
        for qt in range(C.QT):
            cpar = qtglob % 2
            qs = slice(qt * 512, (qt + 1) * 512)
            if qtglob - 2 in ctx_evac_done:
                P.wait("t", "vE", ctx_evac_done[qtglob - 2])
            base_se = P.val("sE")
            base_tm = P.val("tM")
            base_pv = P.val("pV")

            def mk_smm(i):
                def smm(e, hb=hb, sp=i % 2, i=i, qs=qs):
                    e.matmul(ps_s[sp][:, 0:512], ktb[hb][:, (2 * i) * 128:(2 * i + 1) * 128],
                             qtb[hb][:, qs], start=True, stop=True)
                    return e.matmul(ps_s[sp][:, 512:1024],
                                    ktb[hb][:, (2 * i + 1) * 128:(2 * i + 2) * 128],
                                    qtb[hb][:, qs], start=True, stop=True)
                return smm

            if base_se > 0:
                P.wait("t", "sE", base_se - 1)
            P.ein("t", mk_smm(0), "tM")
            if base_se > 0:
                P.wait("t", "sE", base_se)
            P.ein("t", mk_smm(1), "tM")
            for i in range(NPAIR):
                sp = i % 2
                P.wait("s", "tM", base_tm + i + 1)
                if i >= 2:
                    P.wait("s", "pV", base_pv + i - 1)
                P.ein("s", lambda e, sp=sp, h=h: e.activation(
                    pbuf[sp][:], ps_s[sp][:], AF.Exp, bias=bexp_all[:, h:h + 1], scale=1.0), "sE")
                P.wait("t", "sE", base_se + i + 1)
                def pvmm(e, hb=hb, sp=sp, i=i, cpar=cpar, first=(i == 0), last=(i == NPAIR - 1)):
                    e.matmul(ps_c[cpar][:], vtb[hb][:, (2 * i) * 128:(2 * i + 1) * 128],
                             pbuf[sp][:, 0:512], start=first, stop=False)
                    e.matmul(ps_c[cpar][:], vtb[hb][:, (2 * i + 1) * 128:(2 * i + 2) * 128],
                             pbuf[sp][:, 512:1024], start=False, stop=last)
                    e.matmul(ps_l[cpar][0:1, :], ones_bf[:], pbuf[sp][:, 0:512],
                             start=first, stop=last, tile_position=(0, 0))
                    return e.matmul(ps_l[cpar][32:33, :], ones_bf[:], pbuf[sp][:, 512:1024],
                                    start=first, stop=last, tile_position=(0, 32))
                P.ein("t", pvmm, "pV")
                if i + 2 < NPAIR:
                    P.ein("t", mk_smm(i + 2), "tM")
            P.wait("v", "pV", P.val("pV"))
            def levac(e, cpar=cpar):
                e.tensor_copy(l_sb[cpar][:], ps_l[cpar][0:1, :])
                e.drain()
                e.tensor_copy(l_tmp[:], ps_l[cpar][32:33, :])
                e.drain()
                e.tensor_tensor(l_sb[cpar][:], l_sb[cpar][:], l_tmp[:], ALU.add)
                e.drain()
                e.reciprocal(linv_sb[cpar][:], l_sb[cpar][:])
                return e.drain()
            P.ein("v", levac, "vE")
            P.wait("g", "vE", P.val("vE"))
            dlw = P.dma("g", linv_d[cpar:cpar + 1, :], linv_sb[cpar][:], sem=flv[cpar])
            P.wait("g", flv[cpar], dlw)
            dbc = P.dma("g", linv_bc[cpar][:], linv_d[cpar:cpar + 1, :].broadcast_to([128, 512]),
                        sem=flb[cpar])
            P.wait("v", flb[cpar], dbc)
            ctx_evac_done[qtglob] = P.ein("v", lambda e, cpar=cpar, h=h, qs=qs: e.tensor_tensor(
                ctxT[h][:, qs], ps_c[cpar][:], linv_bc[cpar][:], ALU.mult), "vE")
            qtglob += 1
            emit_dequant(2)
        P.inc("hD", 1)
        P.em("t", lambda e: e.nop().then_inc(P.sem["hD"], 1))

    if "ctx_d" in debug:
        P.wait("g", "vE", P.val("vE"))
        for h in range(C.NH):
            P.dma("g", ctx_d[h * 128:(h + 1) * 128, :], ctxT[h][:], sem="dF")
    emit_dequant(len(dq_chunks))
    p3ps.close()
    p3.close()
    p5b.close()
    P.barrier()
    if stop_after < 4:
        return nc, P, st

    p4 = ExitStack()
    wo_sb = p4.enter_context(nc.sbuf_tensor("wo_sb", [128, C.DC * C.D], BF16))
    ob_bc = p4.enter_context(nc.sbuf_tensor("ob_bc", [128, C.D], F32))
    arow = [p4.enter_context(nc.sbuf_tensor(f"arow{i}", [128, C.D], F32)) for i in range(2)]
    xo_b = [p4.enter_context(nc.sbuf_tensor(f"xo_b{i}", [128, C.D], F32)) for i in range(2)]
    xn_b = [p4.enter_context(nc.sbuf_tensor(f"xn_b{i}", [128, C.D], F32)) for i in range(2)]
    ln2b = [p4.enter_context(nc.sbuf_tensor(f"ln2b{i}", [128, C.D], BF16)) for i in range(2)]
    lnat1 = p4.enter_context(nc.sbuf_tensor("lnat1", [128, C.D], F32))
    lnat = [lnat1, lnat1]
    s24d = [p4.enter_context(nc.sbuf_tensor(f"s24d{i}", [128, 6 * C.OG], F32)) for i in range(2)]
    mv2d = [p4.enter_context(nc.sbuf_tensor(f"mv2d{i}", [128, 2], F32)) for i in range(2)]
    sd1d = [p4.enter_context(nc.sbuf_tensor(f"sd1d{i}", [128, 1], F32)) for i in range(2)]
    rs1d = [p4.enter_context(nc.sbuf_tensor(f"rs1d{i}", [128, 1], F32)) for i in range(2)]
    nb1d = [p4.enter_context(nc.sbuf_tensor(f"nb1d{i}", [128, 1], F32)) for i in range(2)]
    s24b = [p4.enter_context(nc.sbuf_tensor(f"s24b{i}", [128, 6 * C.OG], F32)) for i in range(2)]
    mv2b = [p4.enter_context(nc.sbuf_tensor(f"mv2b{i}", [128, 2], F32)) for i in range(2)]
    sd1b = [p4.enter_context(nc.sbuf_tensor(f"sd1b{i}", [128, 1], F32)) for i in range(2)]
    rs1b = [p4.enter_context(nc.sbuf_tensor(f"rs1b{i}", [128, 1], F32)) for i in range(2)]
    nb1b = [p4.enter_context(nc.sbuf_tensor(f"nb1b{i}", [128, 1], F32)) for i in range(2)]
    po = [p4.enter_context(nc.psum_tensor(f"po{i}", [128, 512], F32)) for i in range(2)]

    fw4 = P.fam("fw4", 1)
    fxo = P.fam("fxo", 2)
    fxn = P.fam("fxn", 2)
    f2s = P.fam("f2s", 2)
    P.dma_chunked("g", wo_sb, wo_t, fw4[0], 128, C.D, C.DC, max_desc=512)
    dob = P.dma("g", ob_bc[:], ob_r[:].broadcast_to([128, C.D]), sem=fw4[0])
    P.wait("t", fw4[0], dob)
    P.wait("v", fw4[0], dob)

    st1 = {}
    st2 = {}
    st3 = {}

    def p4_s1(tc):
        p = tc % 2
        if tc - 2 in st2:
            P.wait("g", "vE", st2[tc - 2]["vxn"])
        dxo = P.dma("g", xo_b[p][:], x_full[tc * 128:(tc + 1) * 128, :], sem=fxo[p])
        for og in range(C.OG):
            u = tc * C.OG + og
            pp = po[u % 2]
            if u - 2 in p4_ev_done:
                P.wait("t", "vQ", p4_ev_done[u - 2])
            if tc - 2 in st1:
                pass
            def omm(e, tc=tc, og=og, pp=pp):
                last = None
                for dc in range(C.DC):
                    last = e.matmul(pp[:], ctxT[dc][:, tc * 128:(tc + 1) * 128],
                                    wo_sb[:, dc * C.D + og * 512: dc * C.D + (og + 1) * 512],
                                    start=(dc == 0), stop=(dc == C.DC - 1))
                return last
            mmv = P.ein("t", omm, "tM")
            P.wait("v", "tM", mmv)
            p4_ev_done[u] = P.ein("v", lambda e, pp=pp, og=og, p=p: e.tensor_tensor(
                arow[p][:, og * 512:(og + 1) * 512], pp[:],
                ob_bc[:, og * 512:(og + 1) * 512], ALU.add), "vQ")
        def vst1(e, p=p):
            e.drain()
            for j in range(C.OG):
                e.bn_stats(s24b[p][:, j * 6:(j + 1) * 6], arow[p][:, j * 512:(j + 1) * 512])
            e.drain()
            e.bn_aggr(mv2b[p][:], s24b[p][:])
            return e.drain()
        v1 = P.ein("v", vst1, "vS")
        P.wait("s", "vS", v1)
        sq1 = P.ein("s", lambda e, p=p: e.activation(
            sd1b[p][:], mv2b[p][:, 1:2], AF.Sqrt, bias=eps_t[:, 0:1], scale=1.0), "sS")
        st1[tc] = dict(dxo=dxo, sq1=sq1)

    def p4_s2(tc):
        p = tc % 2
        P.wait("v", "sS", st1[tc]["sq1"])
        P.wait("v", fxo[p], st1[tc]["dxo"])
        if tc - 2 in st2:
            P.wait("v", fxn[p], st2[tc - 2]["xst"])
        def vf1(e, p=p):
            e.reciprocal(rs1b[p][:], sd1b[p][:])
            e.drain()
            e.tensor_scalar(nb1b[p][:], mv2b[p][:, 0:1], rs1b[p][:, 0:1], -1.0, ALU.mult, ALU.mult)
            e.drain()
            e.tensor_scalar(lnat[p][:], arow[p][:], rs1b[p][:, 0:1], nb1b[p][:, 0:1],
                            ALU.mult, ALU.add)
            e.drain()
            e.tensor_tensor(xn_b[p][:], xo_b[p][:], lnat[p][:], ALU.add)
            e.drain()
            for j in range(C.OG):
                e.bn_stats(s24d[p][:, j * 6:(j + 1) * 6], xn_b[p][:, j * 512:(j + 1) * 512])
            e.drain()
            e.bn_aggr(mv2d[p][:], s24d[p][:])
            return e.drain()
        vxn = P.ein("v", vf1, "vE")
        P.wait("g", "vE", vxn)
        xst = P.dma("g", xnew_d[tc * 128:(tc + 1) * 128, :], xn_b[p][:], sem=fxn[p])
        P.wait("s", "vE", vxn)
        sq2 = P.ein("s", lambda e, p=p: e.activation(
            sd1d[p][:], mv2d[p][:, 1:2], AF.Sqrt, bias=eps_t[:, 0:1], scale=1.0), "sS")
        st2[tc] = dict(vxn=vxn, xst=xst, sq2=sq2)

    def p4_s3(tc):
        p = tc % 2
        P.wait("v", "sS", st2[tc]["sq2"])
        if tc - 2 in st3:
            P.wait("v", f2s[p], st3[tc - 2])
        def vf2(e, p=p):
            e.reciprocal(rs1d[p][:], sd1d[p][:])
            e.drain()
            e.tensor_scalar(nb1d[p][:], mv2d[p][:, 0:1], rs1d[p][:, 0:1], -1.0, ALU.mult, ALU.mult)
            e.drain()
            e.tensor_scalar(ln2b[p][:], xn_b[p][:], rs1d[p][:, 0:1], nb1d[p][:, 0:1],
                            ALU.mult, ALU.add)
            return e.drain()
        v2 = P.ein("v", vf2, "vS")
        P.wait("g", "vS", v2)
        st3[tc] = P.dma("g", ln2x_d[tc * 128:(tc + 1) * 128, :], ln2b[p][:], sem=f2s[p])

    p4_ev_done = {}
    for tc in range(C.TCO):
        p4_s1(tc)
        if tc >= 1:
            p4_s2(tc - 1)
        if tc >= 2:
            p4_s3(tc - 2)
    p4_s2(C.TCO - 1)
    p4_s3(C.TCO - 2)
    p4_s3(C.TCO - 1)
    p4.close()
    p34.close()
    P.barrier()
    if stop_after < 5:
        return nc, P, st

    P.barrier()
    if stop_after < 6:
        return nc, P, st

    p6 = ExitStack()
    lt2 = p6.enter_context(nc.sbuf_tensor("lt2", [128, C.DC * C.T_OWN], BF16))
    au_sb = p6.enter_context(nc.sbuf_tensor("au_sb", [128, C.DC * C.R], BF16))
    bu_sb = p6.enter_context(nc.sbuf_tensor("bu_sb", [C.R, C.HID], BF16))
    bup_sb = p6.enter_context(nc.sbuf_tensor("bup_sb", [128, C.HC], F32))
    xa_sb = p6.enter_context(nc.sbuf_tensor("xa_sb", [C.R, C.T_OWN], BF16))
    wu_sl = [p6.enter_context(nc.sbuf_tensor(f"wu_sl{i}", [128, C.DC * 512], BF16)) for i in range(2)]
    h2b = [p6.enter_context(nc.sbuf_tensor(f"h2b{i}", [128, 512], BF16)) for i in range(3)]
    pxa = p6.enter_context(nc.psum_tensor("pxa", [C.R, 512], F32))
    pu = [p6.enter_context(nc.psum_tensor(f"pu{i}", [128, 512], F32)) for i in range(2)]

    f6i = P.fam("f6i", 1)
    f6y = P.fam("f6y", 1)
    fwu = P.fam("fwu", 2)
    fh2 = P.fam("fh2", 3)
    dl2v = None
    for dc in range(C.DC):
        for j0 in range(0, C.T_OWN, 1024):
            w = min(1024, C.T_OWN - j0)
            dl2v = P.dmaT(lt2[:, dc * C.T_OWN + j0: dc * C.T_OWN + j0 + w],
                          ln2x_d[j0:j0 + w, dc * 128:(dc + 1) * 128], sem=f6y[0])
    P.dma_chunked("g", au_sb, a_up, f6i[0], 128, C.R, C.DC, max_desc=512)
    P.dma("g", bu_sb[:], b_upl[:], sem=f6i[0])
    dbb = P.dma("g", bup_sb[:], bup_t[:], sem=f6i[0])
    P.wait("t", f6y[0], dl2v)
    P.wait("t", f6i[0], dbb)
    P.wait("s", f6i[0], dbb)

    for t5 in range(C.QT):
        if t5 >= 1:
            P.wait("t", "vQ", P.val("vQ"))
        def xamm(e, t5=t5):
            last = None
            for dc in range(C.DC):
                last = e.matmul(pxa[:], au_sb[:, dc * C.R:(dc + 1) * C.R],
                                lt2[:, dc * C.T_OWN + t5 * 512: dc * C.T_OWN + (t5 + 1) * 512],
                                start=(dc == 0), stop=(dc == C.DC - 1))
            return last
        xv = P.ein("t", xamm, "tM")
        P.wait("v", "tM", xv)
        P.ein("v", lambda e, t5=t5: e.tensor_copy(xa_sb[:, t5 * 512:(t5 + 1) * 512], pxa[:]), "vQ")
    P.wait("t", "vQ", P.val("vQ"))

    p6_ev = {}
    p6_st = {}
    p6_hg_end = {}
    uu = 0
    for hg in range(C.HG):
        wp = hg % 2
        if hg - 2 in p6_hg_end:
            P.wait("y", "tM", p6_hg_end[hg - 2])
        dwv = None
        for dc in range(C.DC):
            dwv = P.dmaT(wu_sl[wp][:, dc * 512:(dc + 1) * 512],
                         wup_deq_d[hg * 512:(hg + 1) * 512, dc * 128:(dc + 1) * 128],
                         sem=fwu[wp])
        P.wait("t", fwu[wp], dwv)
        for hi in range(4):
            hc = hg * 4 + hi
            for t5 in range(C.QT):
                u = uu
                uu += 1
                pp = pu[u % 2]
                if u - 2 in p6_ev:
                    P.wait("t", "sQ", p6_ev[u - 2])
                def umm(e, wp=wp, hi=hi, t5=t5, pp=pp, hc=hc):
                    for dc in range(C.DC):
                        e.matmul(pp[:],
                                 wu_sl[wp][:, dc * 512 + hi * 128: dc * 512 + (hi + 1) * 128],
                                 lt2[:, dc * C.T_OWN + t5 * 512: dc * C.T_OWN + (t5 + 1) * 512],
                                 start=(dc == 0), stop=False)
                    return e.matmul(pp[:], bu_sb[:, hc * 128:(hc + 1) * 128],
                                    xa_sb[:, t5 * 512:(t5 + 1) * 512], start=False, stop=True)
                mmv = P.ein("t", umm, "tM")
                P.wait("s", "tM", mmv)
                if u - 3 in p6_st:
                    P.wait("s", fh2[(u - 3) % 3], p6_st[u - 3])
                hb2 = h2b[u % 3]
                p6_ev[u] = P.ein("s", lambda e, pp=pp, hb2=hb2, hc=hc: e.activation(
                    hb2[:], pp[:], AF.Relu, bias=bup_sb[:, hc:hc + 1], scale=1.0), "sQ")
                P.wait("g", "sQ", p6_ev[u])
                p6_st[u] = P.dma("g", h2_d[hc * 128:(hc + 1) * 128, t5 * 512:(t5 + 1) * 512],
                                 hb2[:], sem=fh2[u % 3])
        p6_hg_end[hg] = P.val("tM")
    p6.close()
    P.barrier()
    if stop_after < 7:
        return nc, P, st

    p7 = ExitStack()
    TH = C.T_OWN // 2
    h2h = p7.enter_context(nc.sbuf_tensor("h2h", [128, C.HC * TH], BF16))
    ad_sb = p7.enter_context(nc.sbuf_tensor("ad_sb", [128, C.HC * C.R], BF16))
    bd_sb = p7.enter_context(nc.sbuf_tensor("bd_sb", [C.R, C.D], BF16))
    bdn_bc = p7.enter_context(nc.sbuf_tensor("bdn_bc", [128, C.D], F32))
    xb_sb = p7.enter_context(nc.sbuf_tensor("xb_sb", [C.R, TH], BF16))
    wd_sl = [p7.enter_context(nc.sbuf_tensor(f"wd_sl{i}", [128, 512], BF16)) for i in range(4)]
    y2b = [p7.enter_context(nc.sbuf_tensor(f"y2b{i}", [128, 512], F32)) for i in range(3)]

    f7i = P.fam("f7i", 1)
    f7h = P.fam("f7h", 1)
    fwd = P.fam("fwd", 4)
    fy2 = P.fam("fy2", 3)
    P.dma_chunked("g", ad_sb, a_dn, f7i[0], 128, C.R, C.HC, max_desc=512)
    P.dma("g", bd_sb[:], b_dnl[:], sem=f7i[0])
    dbc2 = P.dma("g", bdn_bc[:], bdn_r[:].broadcast_to([128, C.D]), sem=f7i[0])
    P.wait("t", f7i[0], dbc2)
    P.wait("v", f7i[0], dbc2)

    NTI = TH // 128
    for th in range(2):
        if th > 0:
            P.wait("g", "tM", P.val("tM"))
            P.wait("y", "tM", P.val("tM"))
            for sy in fy2:
                P.wait("v", sy, P.val(sy))
        dh2 = None
        for hc in range(C.HC):
            dh2 = P.dma("g", h2h[:, hc * TH:(hc + 1) * TH],
                        h2_d[hc * 128:(hc + 1) * 128, th * TH:(th + 1) * TH], sem=f7h[0])
        P.wait("t", f7h[0], dh2)
        pxs = ExitStack()
        pxb = pxs.enter_context(nc.psum_tensor(f"pxb{th}", [C.R, 512], F32))
        for ts0 in range(0, TH, 512):
            w = min(512, TH - ts0)
            if ts0 > 0:
                P.wait("t", "vQ", P.val("vQ"))
            def xbmm(e, ts0=ts0, w=w):
                last = None
                for hc in range(C.HC):
                    last = e.matmul(pxb[:, 0:w], ad_sb[:, hc * C.R:(hc + 1) * C.R],
                                    h2h[:, hc * TH + ts0: hc * TH + ts0 + w],
                                    start=(hc == 0), stop=(hc == C.HC - 1))
                return last
            xv = P.ein("t", xbmm, "tM")
            P.wait("v", "tM", xv)
            P.ein("v", lambda e, ts0=ts0, w=w: e.tensor_copy(
                xb_sb[:, ts0:ts0 + w], pxb[:, 0:w]), "vQ")
        P.wait("t", "vQ", P.val("vQ"))
        pxs.close()
        pds = ExitStack()
        pd = [pds.enter_context(nc.psum_tensor(f"pd{th}_{i}", [128, 512], F32)) for i in range(NTI)]
        wd_mm_end = {}
        y2_st = {}
        for dg5 in range(C.OG):
            for hc in range(C.HC):
                w = dg5 * C.HC + hc
                wp = w % 4
                if w - 4 in wd_mm_end:
                    P.wait("y", "tM", wd_mm_end[w - 4])
                dwv = P.dma("y", wd_sl[wp][:],
                            wdn_deq_d[hc * 128:(hc + 1) * 128, dg5 * 512:(dg5 + 1) * 512],
                            sem=fwd[wp])
                P.wait("t", fwd[wp], dwv)
                def dmm(e, wp=wp, hc=hc, first=(hc == 0)):
                    last = None
                    for ti in range(NTI):
                        last = e.matmul(pd[ti][:],
                                        h2h[:, hc * TH + ti * 128: hc * TH + (ti + 1) * 128],
                                        wd_sl[wp][:], start=first, stop=False)
                    return last
                wd_mm_end[w] = P.ein("t", dmm, "tM")
            def lmm(e, dg5=dg5):
                last = None
                for ti in range(NTI):
                    last = e.matmul(pd[ti][:], xb_sb[:, ti * 128:(ti + 1) * 128],
                                    bd_sb[:, dg5 * 512:(dg5 + 1) * 512], start=False, stop=True)
                return last
            lv = P.ein("t", lmm, "tM")
            P.wait("v", "tM", lv)
            for ti in range(NTI):
                uy = dg5 * NTI + ti
                if uy - 3 in y2_st:
                    P.wait("v", fy2[(uy - 3) % 3], y2_st[uy - 3])
                yb = y2b[uy % 3]
                ev = P.ein("v", lambda e, ti=ti, yb=yb, dg5=dg5: e.tensor_tensor(
                    yb[:], pd[ti][:], bdn_bc[:, dg5 * 512:(dg5 + 1) * 512], ALU.add), "vQ")
                P.wait("g", "vQ", ev)
                y2_st[uy] = P.dma("g", y2_d[th * TH + ti * 128: th * TH + (ti + 1) * 128,
                                            dg5 * 512:(dg5 + 1) * 512], yb[:], sem=fy2[uy % 3])
            P.wait("t", "vQ", P.val("vQ"))
        pds.close()
    p7.close()
    P.barrier()
    if stop_after < 8:
        return nc, P, st

    NP8 = 4
    p8 = ExitStack()
    y2t = [p8.enter_context(nc.sbuf_tensor(f"y2t{i}", [128, C.D], F32)) for i in range(NP8)]
    xnt = [p8.enter_context(nc.sbuf_tensor(f"xnt{i}", [128, C.D], F32)) for i in range(NP8)]
    lnf = [p8.enter_context(nc.sbuf_tensor(f"lnf{i}", [128, C.D], F32)) for i in range(NP8)]
    otb = [p8.enter_context(nc.sbuf_tensor(f"otb{i}", [128, C.D], F32)) for i in range(NP8)]
    s24c = [p8.enter_context(nc.sbuf_tensor(f"s24c{i}", [128, 6 * C.OG], F32)) for i in range(NP8)]
    mv2c = [p8.enter_context(nc.sbuf_tensor(f"mv2c{i}", [128, 2], F32)) for i in range(NP8)]
    sd1c = [p8.enter_context(nc.sbuf_tensor(f"sd1c{i}", [128, 1], F32)) for i in range(NP8)]
    rs1c = [p8.enter_context(nc.sbuf_tensor(f"rs1c{i}", [128, 1], F32)) for i in range(NP8)]
    nb1c = [p8.enter_context(nc.sbuf_tensor(f"nb1c{i}", [128, 1], F32)) for i in range(NP8)]

    f8l = P.fam("f8l", NP8)
    f8o = P.fam("f8o", NP8)
    p8_done = {}
    p8_fv = {}
    p8_stat = {}
    p8_sqrt = {}

    def p8_stage1(tc):
        p = tc % NP8
        if tc >= NP8:
            P.wait("y", f8o[p], p8_done[tc - NP8])
            P.wait("y", "vE", p8_fv[tc - NP8])
        dy2 = P.dma("y", y2t[p][:], y2_d[tc * 128:(tc + 1) * 128, :], sem=f8l[p])
        dxn = P.dma("y", xnt[p][:], xnew_d[tc * 128:(tc + 1) * 128, :], sem=f8l[p])
        P.wait("v", f8l[p], dxn)
        def vstats(e, p=p):
            for j in range(C.OG):
                e.bn_stats(s24c[p][:, j * 6:(j + 1) * 6], y2t[p][:, j * 512:(j + 1) * 512])
            e.drain()
            e.bn_aggr(mv2c[p][:], s24c[p][:])
            return e.drain()
        p8_stat[tc] = P.ein("v", vstats, "vS")
        P.wait("s", "vS", p8_stat[tc])
        p8_sqrt[tc] = P.ein("s", lambda e, p=p: e.activation(
            sd1c[p][:], mv2c[p][:, 1:2], AF.Sqrt, bias=eps_t[:, 0:1], scale=1.0), "sS")

    def p8_stage2(tc):
        p = tc % NP8
        P.wait("v", "sS", p8_sqrt[tc])
        def vfin(e, p=p):
            e.reciprocal(rs1c[p][:], sd1c[p][:])
            e.drain()
            e.tensor_scalar(nb1c[p][:], mv2c[p][:, 0:1], rs1c[p][:, 0:1], -1.0, ALU.mult, ALU.mult)
            e.drain()
            e.tensor_scalar(lnf[p][:], y2t[p][:], rs1c[p][:, 0:1], nb1c[p][:, 0:1], ALU.mult, ALU.add)
            e.drain()
            e.tensor_tensor(otb[p][:], xnt[p][:], lnf[p][:], ALU.add)
            return e.drain()
        p8_fv[tc] = P.ein("v", vfin, "vE")
        P.wait("g", "vE", p8_fv[tc])
        p8_done[tc] = P.dma("g", out[tc * 128:(tc + 1) * 128, :], otb[p][:], sem=f8o[p])

    for tc in range(C.TCO):
        p8_stage1(tc)
        if tc >= 2:
            p8_stage2(tc - 2)
    p8_stage2(C.TCO - 2)
    p8_stage2(C.TCO - 1)
    for s8 in f8o:
        P.wait("g", s8, P.val(s8))
    return nc, P, st


def build(cfg=None, debug=(), stop_after=99):
    cfg = cfg or Cfg()
    nc, P, st = build_nc(cfg, debug=debug, stop_after=stop_after)
    P.emit()
    return nc


def rope_tables(positions, scale):
    import ml_dtypes
    dh = 128
    inv = 1.0 / (10000.0 ** (np.arange(0, dh, 2, dtype=np.float64) / dh))
    freqs = np.outer(positions.astype(np.float64), inv)
    emb = np.concatenate([freqs, freqs], axis=-1)
    cos = np.cos(emb) * scale
    sin = np.sin(emb) * scale
    sin_s = sin.copy()
    sin_s[:, :dh // 2] *= -1.0
    return (cos.astype(np.float32).astype(ml_dtypes.bfloat16),
            sin_s.astype(np.float32).astype(ml_dtypes.bfloat16))


def make_core_inputs(cfg, g, x_batch, W):
    C = cfg
    T = C.T_OWN
    if g == 0:
        xr = x_batch
        pos = np.arange(C.T_FULL)
    else:
        xr = np.concatenate([x_batch[T:], x_batch[:T]], axis=0)
        pos = np.concatenate([np.arange(T, C.T_FULL), np.arange(T)])
    cq, sq = rope_tables(pos[:T], C.SCL)
    ck, sk = rope_tables(pos, 1.0)
    return dict(
        x_full=np.ascontiguousarray(xr),
        wq_t=np.ascontiguousarray(W["q_w"].T), wk_t=np.ascontiguousarray(W["k_w"].T),
        wv_t=np.ascontiguousarray(W["v_w"].T), wo_t=np.ascontiguousarray(W["out_w"].T),
        qb_r=np.ascontiguousarray(W["q_b"].reshape(1, -1)),
        kb_r=np.ascontiguousarray(W["k_b"].reshape(1, -1)),
        vb_r=np.ascontiguousarray(W["v_b"].reshape(1, -1)),
        ob_r=np.ascontiguousarray(W["out_b"].reshape(1, -1)),
        cq_in=np.ascontiguousarray(cq), sq_in=np.ascontiguousarray(sq),
        ck_in=np.ascontiguousarray(ck), sk_in=np.ascontiguousarray(sk),
        wupq=np.ascontiguousarray(W["w_up_q"]), wups=np.ascontiguousarray(W["w_up_scale"]),
        bup_t=np.ascontiguousarray(W["b_up"].reshape(C.HC, 128).T),
        wdnq_t=np.ascontiguousarray(W["w_down_q"].T),
        wdnsc_t=np.ascontiguousarray(W["w_down_scale"].T),
        bdn_r=np.ascontiguousarray(W["b_down"].reshape(1, -1)),
        a_up=np.ascontiguousarray(W["w_up_lora_a"]), b_upl=np.ascontiguousarray(W["w_up_lora_b"]),
        a_dn=np.ascontiguousarray(W["w_down_lora_a"]), b_dnl=np.ascontiguousarray(W["w_down_lora_b"]),
        ident=np.eye(128, dtype=np.float32),
    )


_NC_CACHE = {}


def kernel(**inputs):
    from concourse.bass_utils import run_bass_kernel_spmd
    cfg = Cfg(T_OWN=2048, T_FULL=4096, D=2048, NH=16, HID=8192, R=16)
    W = dict(
        q_w=inputs["L0_attn_q_w"], k_w=inputs["L0_attn_k_w"], v_w=inputs["L0_attn_v_w"],
        out_w=inputs["L0_attn_out_w"], q_b=inputs["L0_attn_q_b"], k_b=inputs["L0_attn_k_b"],
        v_b=inputs["L0_attn_v_b"], out_b=inputs["L0_attn_out_b"],
        w_up_q=inputs["L0_ffn_w_up_q"], w_up_scale=inputs["L0_ffn_w_up_scale"],
        b_up=inputs["L0_ffn_b_up"], w_up_lora_a=inputs["L0_ffn_w_up_lora_a"],
        w_up_lora_b=inputs["L0_ffn_w_up_lora_b"], w_down_q=inputs["L0_ffn_w_down_q"],
        w_down_scale=inputs["L0_ffn_w_down_scale"], b_down=inputs["L0_ffn_b_down"],
        w_down_lora_a=inputs["L0_ffn_w_down_lora_a"], w_down_lora_b=inputs["L0_ffn_w_down_lora_b"],
    )
    W = {k: np.asarray(v) for k, v in W.items()}
    x = np.asarray(inputs["L0_attn_x"])
    B = x.shape[0]

    if "nc" not in _NC_CACHE:
        _NC_CACHE["nc"] = build(cfg)
    nc = _NC_CACHE["nc"]

    shared = None
    in_maps = []
    for c in range(2 * B):
        b, g = c // 2, c % 2
        m = make_core_inputs(cfg, g, x[b], W)
        if shared is None:
            shared = {k: m[k] for k in ("wq_t", "wk_t", "wv_t", "wo_t", "qb_r", "kb_r",
                                        "vb_r", "ob_r", "wupq", "wups", "bup_t", "wdnq_t",
                                        "wdnsc_t", "bdn_r", "a_up", "b_upl", "a_dn", "b_dnl",
                                        "ident")}
        else:
            m.update(shared)
        in_maps.append(m)

    res = run_bass_kernel_spmd(nc, in_maps, core_ids=list(range(2 * B)))
    out = np.empty((B, cfg.T_FULL, cfg.D), dtype=np.float32)
    for c in range(2 * B):
        b, g = c // 2, c % 2
        out[b, g * cfg.T_OWN:(g + 1) * cfg.T_OWN] = np.asarray(res.results[c]["out"])
    return out
